# version 07
# baseline (speedup 1.0000x reference)
import numpy as np
import concourse.bass as bass
import concourse.mybir as mybir
from concourse.ap import AP
from concourse.bass_utils import run_bass_kernel_spmd

F32 = mybir.dt.float32
BF16 = mybir.dt.bfloat16
ALU = mybir.AluOpType
AF = mybir.ActivationFunctionType
AXX = mybir.AxisListType.X

D = 1024
H = 16
FT = 16
WIN = 4096
OWN = 2048
NCH = 32
NBIG = 8
NSUB = 4
DE = 273
HS = 2048
NE = 8
EPS_ATTN = 1e-12
EPS_LN = 1e-5

MOE_RES = []
for e in range(NE):
    r0 = (3 * e) % 8
    r1 = (3 * e + 3) % 8
    MOE_RES.append((min(r0, r1), abs(r1 - r0)))


class KB:
    ENGINES = ("sync", "tensor", "vector", "scalar", "gpsimd")

    def __init__(self, nc):
        self.nc = nc
        self.ops = {e: [] for e in self.ENGINES}
        self.sems = {}
        self.cnt = {}

    def sem(self, name):
        self.sems[name] = self.nc.alloc_semaphore(name)
        self.cnt[name] = 0

    def emit(self, eng, fn, inc=None):
        if inc is not None:
            self.cnt[inc] += 1
            sem = self.sems[inc]
            self.ops[eng].append(lambda e, fn=fn, sem=sem: fn(e).then_inc(sem, 1))
            return self.cnt[inc]
        self.ops[eng].append(fn)
        return None

    def wait(self, eng, sem_name, val):
        if val is None or val <= 0:
            return
        sem = self.sems[sem_name]
        self.ops[eng].append(lambda e, sem=sem, v=val: e.wait_ge(sem, v))

    def now(self, s):
        return self.cnt[s]

    def mark(self, name):
        self.marks = getattr(self, "marks", {})
        self.marks[name] = {e: len(self.ops[e]) for e in self.ENGINES}

    def cut_to(self, name):
        m = self.marks[name]
        for e in self.ENGINES:
            self.ops[e] = self.ops[e][:m[e]]

    def dma(self, eng, out, in_, sem_name):
        self.cnt[sem_name] += 16
        sem = self.sems[sem_name]
        self.ops[eng].append(
            lambda e, out=out, in_=in_, sem=sem: e.dma_start(out=out, in_=in_).then_inc(sem, 16))
        return self.cnt[sem_name]


def ap3(t_ap, base_off, dims):
    return AP(t_ap.tensor, t_ap.offset + base_off,
              [list(t_ap.ap[0])] + [list(d) for d in dims])


def build_program(taps=(), cut=None, no_inter=False, races=False):
    nc = bass.Bass("TRN2", target_bir_lowering=False, detect_race_conditions=races)
    dram = {}

    def din(name, shape):
        dram[name] = nc.dram_tensor(name, list(shape), F32, kind="ExternalInput").ap()
        return dram[name]

    din("x_fm", (D, WIN))
    din("x_tok", (OWN, D))
    din("valid", (128, NCH))
    din("wcat", (D, 1536))
    din("wo_in", (D, D))
    din("sw1", (D, HS))
    din("sw3", (D, HS))
    din("sw2", (HS, D))
    din("sb1", (128, 16))
    din("sb3", (128, 16))
    din("sb2h", (128, 8))
    din("w1", (NE, D, D))
    din("b1", (128, NE, 8))
    din("b2p", (128, 8, 8))
    din("w2", (NE, D, D))
    din("maskh", (128, 128))
    din("ident", (128, 128))
    din("rep4", (16, 128))
    din("sel16", (16, 1024))
    out_d = nc.dram_tensor("out", [OWN, D], F32, kind="ExternalOutput").ap()
    x1_scr = nc.dram_tensor("x1_scr", [OWN, D], F32, kind="Internal").ap()

    from contextlib import ExitStack
    ctx = ExitStack()

    def sb(name, shape, dt=BF16):
        return ctx.enter_context(nc.sbuf_tensor(name, list(shape), dt)).ap()

    maskh_b = sb("maskh_b", (128, 128))
    ident_b = sb("ident_b", (128, 128))
    ident_f = sb("ident_f", (128, 128), F32)
    rep4_b = sb("rep4_b", (16, 128))
    sel16_b = sb("sel16_b", (16, 1024))
    ones_row = sb("ones_row", (1, 128))
    valid_s = sb("valid_s", (128, NCH), F32)
    sb1_s = sb("sb1_s", (128, 16), F32)
    sb3_s = sb("sb3_s", (128, 16), F32)
    sb2h_s = sb("sb2h_s", (128, 8), F32)
    b1_s = sb("b1_s", (128, NE * 8), F32)
    b2p_s = sb("b2p_s", (128, NE * 8), F32)
    xtok_b = sb("xtok_b", (128, 2 * D), F32)
    h_b = sb("h_b", (128, 2 * D), F32)
    dump_b = sb("dump_b", (128, D), F32)
    x1bf_b = sb("x1bf_b", (128, 2 * D))
    x2o_b = sb("x2o_b", (128, 2 * D), F32)
    m_b = sb("m_b", (128, 2), F32)
    mneg_b = sb("mneg_b", (128, 2), F32)
    var_b = sb("var_b", (128, 2), F32)
    rstd_b = sb("rstd_b", (128, 2), F32)
    recip_s = sb("recip_s", (128, 16), F32)
    eps_b = sb("eps_b", (128, 1), F32)

    AR_BYTES = 175424
    AR = sb("arena", (128, AR_BYTES // 4), F32)
    ARB = AR.bitcast(BF16)

    off = {}

    def lay(name, nbytes, at=None):
        if at is None:
            at = lay.cur
            lay.cur += (nbytes + 63) // 64 * 64
        off[name] = at
        assert at + nbytes <= AR_BYTES, (name, at, nbytes)
        return at
    lay.cur = 0

    def bfv(name, cols, p0=0, np_=128):
        o = off[name]
        assert o % 2 == 0
        a = ARB[:, o // 2: o // 2 + cols]
        if p0 or np_ != 128:
            a = AP(a.tensor, a.offset + p0 * a.ap[0][0], [[a.ap[0][0], np_]] + [list(d) for d in a.ap[1:]])
        return a

    def f32v(name, cols, p0=0, np_=128):
        o = off[name]
        assert o % 4 == 0
        a = AR[:, o // 4: o // 4 + cols]
        if p0 or np_ != 128:
            a = AP(a.tensor, a.offset + p0 * a.ap[0][0], [[a.ap[0][0], np_]] + [list(d) for d in a.ap[1:]])
        return a

    lay("wcat", 8 * 1536 * 2)
    lay("wo", 8 * D * 2)
    kft_at = lay("kft", NSUB * H * DE * 2)
    lay("qfx2", NSUB * H * 256 * 2)
    lay("vaug", NSUB * H * 65 * 2)
    lay("qT", NSUB * 256 * 2)
    lay("kT", NSUB * 256 * 2)
    lay("qfm", 2 * 512 * 2)
    lay("qfmS", 2 * 512 * 2)
    lay("kfm", 2 * 512 * 2)
    lay("kfmS", 2 * 512 * 2)
    lay("yqfT", max(NSUB * D, 4096) * 2)
    lay("yT", 8 * 128 * 2)
    lay("sq", 512 * 2)
    lay("atm", 1024 * 2)
    lay("s0f", H * 65 * 4)
    lay("s1f", H * 65 * 4)
    lay("s2f", H * 65 * 4)
    lay("scf", H * 65 * 4)
    lay("s0b", H * 65 * 2)
    lay("s1b", H * 65 * 2)
    lay("s2b", H * 65 * 2)
    lay("s2b4", H * 65 * 2)
    lay("scb", H * 65 * 2)
    lay("xfm", 2 * 1024 * 2)
    a_end = lay.cur
    assert a_end <= AR_BYTES, a_end

    lay("x1fm", 8 * OWN * 2, at=kft_at)
    lay("w1s", 8 * D * 2, at=0)
    lay("w2s", 8 * D * 2, at=8 * D * 2)
    lay("sw1s", 2 * 8 * 512 * 2, at=0)
    lay("sw3s", 2 * 8 * 512 * 2, at=8 * D * 2)
    lay("hmoe", 8 * 512 * 2, at=16 * D * 2)
    lay("sw2s", 2 * 16 * 128 * 2, at=16 * D * 2)
    f_cur = kft_at + 8 * OWN * 2
    lay("omoe", 8 * 2048 * 4, at=f_cur); f_cur += 8 * 2048 * 4
    f_cur = max(f_cur, 140544)
    lay("sgate", 16 * 1024 * 2, at=f_cur); f_cur += 16 * 1024 * 2
    lay("sq2", 512 * 2, at=f_cur); f_cur += 512 * 2
    assert f_cur <= AR_BYTES, f_cur

    wcat_b = bfv("wcat", 8 * 1536)
    wo_b = bfv("wo", 8 * D)
    kft_b = bfv("kft", NSUB * H * DE)
    qfx2_b = bfv("qfx2", NSUB * H * 256)
    vaug_b = bfv("vaug", NSUB * H * 65)
    qT_b = bfv("qT", NSUB * 256)
    kT_b = bfv("kT", NSUB * 256)
    qfm_b = bfv("qfm", 2 * 512)
    qfmS_b = bfv("qfmS", 2 * 512)
    kfm_b = bfv("kfm", 2 * 512)
    kfmS_b = bfv("kfmS", 2 * 512)
    y_b = bfv("yqfT", NSUB * D)
    qfT_b = bfv("yqfT", 4096)
    yT_b = bfv("yT", 8 * 128)
    sq_b = bfv("sq", 512)
    atm_b = bfv("atm", 1024)
    s0f = f32v("s0f", H * 65)
    s1f = f32v("s1f", H * 65)
    s2f = f32v("s2f", H * 65, 0, 16)
    scf = f32v("scf", H * 65, 0, 1)
    s0b = bfv("s0b", H * 65)
    s1b = bfv("s1b", H * 65)
    s2b = bfv("s2b", H * 65, 0, 16)
    s2stk = bfv("s2b4", H * 65)
    scb = bfv("scb", H * 65, 0, 1)
    xfm_b = bfv("xfm", 2 * 1024)
    x1fm_b = bfv("x1fm", 8 * OWN)
    w1_b = bfv("w1s", 8 * D)
    w2_b = bfv("w2s", 8 * D)
    hmoe_b = bfv("hmoe", 8 * 512)
    omoe_b = f32v("omoe", 8 * 2048)
    sgate_b = bfv("sgate", 16 * 1024)
    sw1_b = bfv("sw1s", 2 * 8 * 512)
    sw3_b = bfv("sw3s", 2 * 8 * 512)
    sw2_b = bfv("sw2s", 2 * 16 * 128)

    ps = [ctx.enter_context(nc.psum_tensor(f"pb{i}", [128, 512], F32)).ap() for i in range(8)]
    psb = [p.bitcast(BF16) for p in ps]

    kb = KB(nc)
    for s in ("T", "V", "S", "G", "DPX", "DPC", "XF0", "XF1", "WM1", "WM2",
              "SWE", "SWO", "W2E", "W2O", "XT0", "XT1", "X1W0", "X1W1",
              "X1R0", "X1R1", "SH", "DO0", "DO1", "DTAP"):
        kb.sem(s)

    kb.mark("zero")
    tap_dmas = []

    def add_tap(name, src_ap, shape):
        if name in taps:
            t = nc.dram_tensor("tap_" + name, list(shape), F32, kind="ExternalOutput").ap()
            tap_dmas.append((t, src_ap))

    tap_fired = set()

    def tap_now(name, src_ap, shape, wait_eng_sem=None):
        if name not in taps or name in tap_fired:
            return
        tap_fired.add(name)
        t = nc.dram_tensor("tap_" + name, list(shape), F32, kind="ExternalOutput").ap()
        if wait_eng_sem is not None:
            kb.wait("gpsimd", wait_eng_sem, kb.now(wait_eng_sem))
        dv = kb.dma("gpsimd", t, src_ap, "DTAP")
        for eng in ("vector", "tensor", "scalar"):
            kb.wait(eng, "DTAP", dv)

    def MM(t, o, l, r, st, sp, tp=None):
        return t.matmul(o, l, r, start=st, stop=sp, tile_position=tp, skip_group_check=True)

    kb.dma("gpsimd", maskh_b[:], dram["maskh"], "DPC")
    kb.dma("gpsimd", ident_b[:], dram["ident"], "DPC")
    kb.dma("gpsimd", rep4_b[:], dram["rep4"], "DPC")
    kb.dma("gpsimd", sel16_b[:], dram["sel16"], "DPC")
    kb.dma("gpsimd", wcat_b.rearrange("p (c n) -> p c n", c=8),
           dram["wcat"].rearrange("(c p) n -> p c n", p=128), "DPC")
    kb.dma("gpsimd", wo_b.rearrange("p (c n) -> p c n", c=8),
           dram["wo_in"].rearrange("(c p) n -> p c n", p=128), "DPC")
    kb.dma("sync", ident_f[:], dram["ident"], "DPX")
    kb.dma("sync", valid_s[:], dram["valid"], "DPX")
    kb.dma("sync", sb1_s[:], dram["sb1"], "DPX")
    kb.dma("sync", sb3_s[:], dram["sb3"], "DPX")
    kb.dma("sync", sb2h_s[:], dram["sb2h"], "DPX")
    kb.dma("sync", b1_s[:].rearrange("p (e c) -> p e c", e=NE), dram["b1"], "DPX")
    kb.dma("sync", b2p_s[:].rearrange("p (c r) -> p c r", c=8), dram["b2p"], "DPX")
    pre_dx, pre_dc = kb.now("DPX"), kb.now("DPC")
    kb.mark("pre_dma")
    kb.wait("vector", "DPX", pre_dx)
    kb.wait("vector", "DPC", pre_dc)
    kb.emit("vector", lambda v: v.memset(ones_row[:], 1.0))
    kb.emit("vector", lambda v: v.memset(eps_b[:], EPS_LN))
    kb.emit("vector", lambda v: v.memset(s0f, 0.0))
    kb.emit("vector", lambda v: v.memset(s1f, 0.0))
    kb.emit("vector", lambda v: v.memset(s2f, 0.0))
    v_pre = kb.emit("vector", lambda v: v.memset(scf, 0.0), inc="V")
    kb.mark("pre_ms")
    kb.wait("tensor", "V", v_pre)
    kb.wait("scalar", "V", v_pre)
    kb.wait("tensor", "DPC", pre_dc)
    kb.wait("tensor", "DPX", pre_dx)
    kb.wait("gpsimd", "V", v_pre)

    kb.mark("pre")
    t_after_proj = {}
    v_after_evac = {}
    ds_after_x1 = {}
    g_after_gen = 0
    v_after_gen = 0
    t_after_ds = 0
    v_after_h = {}

    def ln_tile(src_f32, slot, out_ap, also_bf=None):
        vr = kb.emit("vector", lambda v: v.reduce_sum(m_b[:, slot:slot + 1], src_f32, AXX), inc="V")
        kb.wait("vector", "V", vr)
        vm = kb.emit("vector", lambda v: v.tensor_scalar(
            mneg_b[:, slot:slot + 1], m_b[:, slot:slot + 1], -1.0 / D, None, ALU.mult), inc="V")
        kb.wait("scalar", "V", vm)
        sv = kb.emit("scalar", lambda s_: s_.activation(
            dump_b[:], src_f32, AF.Square, bias=mneg_b[:, slot:slot + 1], scale=1.0), inc="S")
        kb.wait("vector", "S", sv)
        vv2 = kb.emit("vector", lambda v: v.reduce_sum(var_b[:, slot:slot + 1], dump_b[:], AXX), inc="V")
        kb.wait("scalar", "V", vv2)
        sn = kb.emit("scalar", lambda s_: s_.activation(
            rstd_b[:, slot:slot + 1], var_b[:, slot:slot + 1], AF.Ln, bias=eps_b[:], scale=1.0 / D), inc="S")
        kb.wait("scalar", "S", sn)
        sl = kb.emit("scalar", lambda s_: s_.activation(
            rstd_b[:, slot:slot + 1], rstd_b[:, slot:slot + 1], AF.Exp, bias=0.0, scale=-0.5), inc="S")
        kb.wait("vector", "S", sl)
        vap = kb.emit("vector", lambda v: v.tensor_scalar(
            out_ap, src_f32, mneg_b[:, slot:slot + 1], rstd_b[:, slot:slot + 1],
            ALU.add, ALU.mult), inc="V")
        if also_bf is not None:
            vap = kb.emit("vector", lambda v: v.tensor_copy(also_bf, out_ap), inc="V")
        return vap

    tsq_hist = []
    tat_hist = []
    for bc in range(NBIG):
        own = bc >= NBIG // 2

        for sub in range(NSUB):
            c = bc * NSUB + sub
            xr = c % 2
            if c - 2 in t_after_proj:
                kb.wait("gpsimd", "T", t_after_proj[c - 2])
            src = dram["x_fm"][:, c * 128:(c + 1) * 128].rearrange("(cc p) t -> p cc t", p=128)
            xfs = "XF0" if c % 2 == 0 else "XF1"
            dxv = kb.dma("gpsimd",
                         ap3(xfm_b, xr * 1024, [[128, 8], [1, 128]]),
                         src, xfs)
            kb.wait("tensor", xfs, dxv)
            if sub == 0:
                kb.wait("vector", "G", kb.now("G"))
            for ic in range(8):
                lhs = ap3(xfm_b, xr * 1024 + ic * 128, [[1, 128]])
                w0 = ap3(wcat_b, ic * 1536, [[1, 512]])
                w1c = ap3(wcat_b, ic * 1536 + 512, [[1, 512]])
                w2c = ap3(wcat_b, ic * 1536 + 1024, [[1, 512]])
                st, sp = (ic == 0), (ic == 7)
                kb.emit("tensor", lambda t, l=lhs, r=w0, st=st, sp=sp: MM(t, ps[0][:], l, r, st, sp))
                kb.emit("tensor", lambda t, l=lhs, r=w1c, st=st, sp=sp: MM(t, ps[1][:], l, r, st, sp))
                tv = kb.emit("tensor", lambda t, l=lhs, r=w2c, st=st, sp=sp: MM(t, ps[2][:], l, r, st, sp),
                             inc=("T" if sp else None))
            t_after_proj[c] = tv
            kb.wait("vector", "T", tv)
            o_qt = ap3(qT_b, sub * 256, [[1, 256]])
            o_kt = ap3(kT_b, sub * 256, [[1, 256]])
            kb.emit("vector", lambda v, o=o_qt: v.tensor_scalar(o, ps[0][:, 0:256], 0.5, None, ALU.mult))
            kb.emit("vector", lambda v, o=o_kt: v.tensor_scalar(o, ps[0][:, 256:512], 0.35355339059327373, None, ALU.mult))
            o_v0 = ap3(vaug_b, sub * H * 65, [[65, 8], [1, 64]])
            o_v1 = ap3(vaug_b, sub * H * 65 + 8 * 65, [[65, 8], [1, 64]])
            kb.emit("vector", lambda v, o=o_v0: v.tensor_copy(o, ps[1][:].rearrange("p (h e) -> p h e", h=8)))
            kb.emit("vector", lambda v, o=o_v1: v.tensor_copy(o, ps[2][:].rearrange("p (h e) -> p h e", h=8)))
            o_vc = ap3(vaug_b, sub * H * 65 + 64, [[65, H]])
            i_vc = ap3(valid_s, c, [[0, H]])
            vv = kb.emit("vector", lambda v, o=o_vc, i=i_vc: v.tensor_copy(o, i), inc="V")
            v_after_evac[c] = vv
            kb.wait("tensor", "V", vv)
            if c == 16:
                tap_now("qt_c16", ap3(qT_b, 0, [[1, 256]]), (128, 256), "V")
                tap_now("vaug_c16", ap3(vaug_b, 0, [[1, 1040]]), (128, 1040), "V")

            if own:
                for ti in range(2):
                    i_q = ap3(qT_b, sub * 256 + ti * 128, [[1, 128]])
                    i_k = ap3(kT_b, sub * 256 + ti * 128, [[1, 128]])
                    kb.emit("tensor", lambda t, i=i_q, ti=ti: t.transpose(
                        psb[3][:, ti * 128:(ti + 1) * 128], i, ident_b[:]))
                    tv2 = kb.emit("tensor", lambda t, i=i_k, ti=ti: t.transpose(
                        psb[3][:, 256 + ti * 128: 256 + (ti + 1) * 128], i, ident_b[:]),
                        inc=("T" if ti == 1 else None))
                kb.wait("scalar", "T", tv2)
                kb.wait("scalar", "SH", kb.now("SH"))
                o_qfm = ap3(qfm_b, sub * 128, [[512, 2], [1, 128]])
                o_kfm = ap3(kfm_b, sub * 128, [[512, 2], [1, 128]])
                kb.emit("scalar", lambda s_, o=o_qfm: s_.copy(
                    o, psb[3][:, 0:256].rearrange("p (ti t) -> p ti t", ti=2)))
                sv = kb.emit("scalar", lambda s_, o=o_kfm: s_.copy(
                    o, psb[3][:, 256:512].rearrange("p (ti t) -> p ti t", ti=2)), inc="S")
                kb.wait("tensor", "S", sv)
                kb.wait("sync", "S", sv)
                for src_t, dst_t in ((qfm_b, qfmS_b), (kfm_b, kfmS_b)):
                    for ti in range(2):
                        i_s = AP(src_t.tensor, src_t.offset + 16 * src_t.ap[0][0] + ti * 512 + sub * 128,
                                 [[src_t.ap[0][0], 112], [1, 128]])
                        o_s = AP(dst_t.tensor, dst_t.offset + ti * 512 + sub * 128,
                                 [[dst_t.ap[0][0], 112], [1, 128]])
                        kb.dma("sync", o_s, i_s, "SH")

        kb.mark(f"proj{bc}")
        kb.wait("vector", "V", v_after_evac[bc * NSUB + NSUB - 1])
        for sub in range(NSUB):
            c = bc * NSUB + sub
            kbase = sub * H * DE
            kt = ap3(kT_b, sub * 256, [[1, 256]])
            for (eng, h0) in (("vector", 0), ("gpsimd", 8)):
                in0 = ap3(kT_b, sub * 256 + h0 * FT, [[FT, 8], [1, FT], [0, FT]])
                in1 = ap3(kT_b, sub * 256 + h0 * FT, [[FT, 8], [0, FT], [1, FT]])
                outa = ap3(kft_b, kbase + h0 * DE, [[DE, 8], [FT, FT], [1, FT]])
                if eng == "gpsimd":
                    kb.wait("gpsimd", "T", t_after_ds)
                    kb.wait("gpsimd", "V", v_after_evac[c])
                kb.emit(eng, lambda e, o=outa, a=in0, b=in1: e.tensor_tensor(
                    o, a, b, ALU.mult))
            o_lin = ap3(kft_b, kbase + 256, [[DE, H], [1, FT]])
            i_lin = ap3(kT_b, sub * 256, [[FT, H], [1, FT]])
            kb.emit("vector", lambda v, o=o_lin, i=i_lin: v.tensor_scalar(
                o, i, 1.4142135623730951, None, ALU.mult))
            o_cst = ap3(kft_b, kbase + 272, [[DE, H]])
            i_cst = ap3(valid_s, c, [[0, H]])
            kb.emit("vector", lambda v, o=o_cst, i=i_cst: v.tensor_copy(o, i))
        v_after_gen = kb.emit("vector", lambda v: v.engine_nop(), inc="V")
        g_after_gen = kb.emit("gpsimd", lambda g: g.engine_nop(), inc="G")
        if bc == 4:
            kb.wait("gpsimd", "V", v_after_gen)
            tap_now("kft_c16", ap3(kft_b, 0, [[1, 4368]]), (128, 4368))

        kb.mark(f"gen{bc}")
        if own:
            for sub in range(NSUB):
                c = bc * NSUB + sub
                kb.wait("vector", "T", kb.now("T"))
                kb.wait("vector", "V", v_after_evac[c])
                for (eng, h0) in (("vector", 0), ("gpsimd", 8)):
                    in0 = ap3(qT_b, sub * 256 + h0 * FT, [[FT, 8], [1, FT], [0, FT]])
                    in1 = ap3(qT_b, sub * 256 + h0 * FT, [[FT, 8], [0, FT], [1, FT]])
                    outa = ap3(qfT_b, h0 * 256, [[256, 8], [FT, FT], [1, FT]])
                    if eng == "gpsimd":
                        kb.wait("gpsimd", "V", v_after_evac[c])
                        kb.wait("gpsimd", "T", kb.now("T"))
                    kb.emit(eng, lambda e, o=outa, a=in0, b=in1: e.tensor_tensor(o, a, b, ALU.mult))
                qv = kb.emit("vector", lambda v: v.engine_nop(), inc="V")
                qg = kb.emit("gpsimd", lambda g: g.engine_nop(), inc="G")
                kb.wait("tensor", "V", qv)
                kb.wait("tensor", "G", qg)
                grp = [(h, cc) for h in range(H) for cc in range(2)]
                for gi in range(0, 32, 4):
                    for j in range(4):
                        h, cc = grp[gi + j]
                        i_t = ap3(qfT_b, h * 256 + cc * 128, [[1, 128]])
                        tv3 = kb.emit("tensor", lambda t, i=i_t, j=j: t.transpose(
                            psb[4][:, j * 128:(j + 1) * 128], i, ident_b[:]),
                            inc=("T" if j == 3 else None))
                    kb.wait("scalar", "T", tv3)
                    for j in range(4):
                        h, cc = grp[gi + j]
                        o_q = ap3(qfx2_b, (sub * H + h) * 256 + cc * 128, [[1, 128]])
                        sv2 = kb.emit("scalar", lambda s_, j=j, o=o_q: s_.copy(
                            o, psb[4][:, j * 128:(j + 1) * 128]),
                            inc=("S" if j == 3 else None))
                    kb.wait("tensor", "S", sv2)
                if sub == 0 and bc == 4:
                    tap_now("qfx2_c16", ap3(qfx2_b, 0, [[1, 512]]), (128, 512), "S")

            kb.emit("vector", lambda v: v.tensor_copy(s0b, s0f))
            kb.emit("vector", lambda v: v.tensor_copy(s1b, s1f))
            kb.emit("vector", lambda v: v.tensor_copy(s2b, s2f))
            scast = kb.emit("vector", lambda v: v.tensor_copy(scb, scf), inc="V")
            kb.wait("tensor", "V", scast)
            kb.wait("tensor", "SH", kb.now("SH"))
            kb.mark(f"scast{bc}")
            for ti in range(2):
                for ps_ in range(2):
                    for j in range(4):
                        jj = ps_ * 4 + j
                        h = ti * 8 + jj
                        l_sel = sel16_b[:, jj * 128:(jj + 1) * 128]
                        r_s2 = ap3(s2b, h * 65, [[1, 65]])
                        trep = kb.emit("tensor", lambda t, l=l_sel, r=r_s2, j=j: MM(
                            t, ps[2][:, j * 65:(j + 1) * 65], l, r, j == 0, j == 3),
                            inc=("T" if j == 3 else None))
                    kb.wait("scalar", "T", trep)
                    o_st = ap3(s2stk, ti * 520 + ps_ * 260, [[1, 260]])
                    srep = kb.emit("scalar", lambda s_, o=o_st: s_.copy(
                        o, ps[2][:, 0:260]), inc="S")
                    kb.wait("tensor", "S", srep)

            kb.mark(f"qfx{bc}")
            kb.mark(f"rep{bc}")
            for pair in range(2):
                qcs = (0, 1) if pair == 0 else (2, 3)
                qhi = (qcs[-1] + 1) * 128

                def num_ap(h, qi):
                    if h < 7:
                        bank = 3 if qi == 0 else 5
                        return ps[bank][:, h * 66:h * 66 + 65]
                    if h < 14:
                        bank = 4 if qi == 0 else 6
                        return ps[bank][:, (h - 7) * 66:(h - 7) * 66 + 65]
                    return ps[7][:, qi * 132 + (h - 14) * 66: qi * 132 + (h - 14) * 66 + 65]

                strip_off = {}
                soff = 0
                for kc in range(qcs[-1] + 1):
                    strip_off[kc] = soff
                    soff += qhi - max(kc, qcs[0]) * 128
                assert soff <= 1024

                for h in range(H):
                    ti, hh = h // 8, h % 8
                    if hh % 2 == 0:
                        base = hh * 16
                        kfm_t, qfm_t = kfm_b, qfm_b
                    else:
                        base = hh * 16 - 16
                        kfm_t, qfm_t = kfmS_b, qfmS_b
                    tp = (96, 0) if base == 96 else None
                    tat = None
                    for kc in range(qcs[-1] + 1):
                        par = len(tsq_hist) % 2
                        sbank = 2 if par == 0 else 0
                        sqo = par * 256
                        if len(tsq_hist) >= 2:
                            kb.wait("tensor", "S", tsq_hist[-2])
                        if len(tat_hist) >= 2:
                            kb.wait("scalar", "V", tat_hist[-2])
                        sp_lo = max(kc, qcs[0]) * 128
                        nq = qhi - sp_lo
                        l_sc = AP(kfm_t.tensor, kfm_t.offset + base * kfm_t.ap[0][0] + ti * 512 + kc * 128,
                                  [[kfm_t.ap[0][0], 16], [1, 128]])
                        r_sc = AP(qfm_t.tensor, qfm_t.offset + base * qfm_t.ap[0][0] + ti * 512 + sp_lo,
                                  [[qfm_t.ap[0][0], 16], [1, nq]])
                        tsc = kb.emit("tensor", lambda t, l=l_sc, r=r_sc, nq=nq, tp=tp, sbank=sbank:
                                      MM(t, ps[sbank][:, 0:nq], l, r, True, True, tp), inc="T")
                        kb.wait("scalar", "T", tsc)
                        tsq = kb.emit("scalar", lambda s_, nq=nq, sbank=sbank, sqo=sqo: s_.activation(
                            sq_b[:, sqo:sqo + nq], ps[sbank][:, 0:nq], AF.Square, bias=1.0,
                            scale=1.4142135623730951), inc="S")
                        tsq_hist.append(tsq)
                        kb.wait("vector", "S", tsq)
                        has_diag = (sp_lo == kc * 128)
                        so = strip_off[kc]

                        def poly(v, nq=nq, has_diag=has_diag, so=so, sqo=sqo):
                            ins = None
                            col = 0
                            if has_diag:
                                ins = v.scalar_tensor_tensor(atm_b[:, so:so + 128], sq_b[:, sqo:sqo + 128], 1.0,
                                                             maskh_b[:], ALU.add, ALU.mult)
                                col = 128
                            if col < nq:
                                ins = v.tensor_scalar(atm_b[:, so + col:so + nq], sq_b[:, sqo + col:sqo + nq],
                                                      0.5, 0.5, ALU.mult, ALU.add)
                            return ins
                        tat = kb.emit("vector", poly, inc="V")
                        tat_hist.append(tat)
                    kb.wait("tensor", "V", tat)
                    last_mm = None
                    for qi, qc in enumerate(qcs):
                        o = num_ap(h, qi)
                        mms = []
                        lx0 = ap3(qfx2_b, (qc * H + h) * 256, [[1, 128]])
                        lx1 = ap3(qfx2_b, (qc * H + h) * 256 + 128, [[1, 128]])
                        mms.append((lx0, ap3(s0b, h * 65, [[1, 65]])))
                        mms.append((lx1, ap3(s1b, h * 65, [[1, 65]])))
                        lin = ap3(qfm_b, ti * 512 + qc * 128, [[1, 128]])
                        mms.append((lin, ap3(s2stk, ti * 520 + hh * 65, [[1, 65]])))
                        mms.append((ones_row[:, :], ap3(scb, h * 65, [[1, 65]])))
                        for kc in range(qc + 1):
                            so = strip_off[kc] + (qc * 128 - max(kc, qcs[0]) * 128)
                            mms.append((atm_b[:, so:so + 128],
                                        ap3(vaug_b, kc * H * 65 + h * 65, [[1, 65]])))
                        if no_inter:
                            mms = mms[3:]
                        for mi, (l_m, r_m) in enumerate(mms):
                            last_mm = kb.emit("tensor", lambda t, o=o, l=l_m, r=r_m,
                                              st=(mi == 0), sp=(mi == len(mms) - 1):
                                              MM(t, o, l, r, st, sp),
                                              inc=("T" if mi == len(mms) - 1 else None))
                    kb.wait("vector", "T", last_mm)
                tnum = kb.emit("tensor", lambda t: t.nop(), inc="T")
                kb.wait("vector", "T", tnum)
                if bc == 4 and pair == 0 and ("num_a" in taps or "num_7" in taps):
                    kb.wait("scalar", "T", tnum)
                    sda = kb.emit("scalar", lambda s_: s_.copy(dump_b[:, 0:462], ps[3][:, 0:462]), inc="S")
                    kb.wait("gpsimd", "S", sda)
                    tap_now("num_a", dump_b[:, 0:462], (128, 462))
                    sdb = kb.emit("scalar", lambda s_: s_.copy(dump_b[:, 0:264], ps[7][:, 0:264]), inc="S")
                    kb.wait("gpsimd", "S", sdb)
                    tap_now("num_7", dump_b[:, 0:264], (128, 264))
                for qi, qc in enumerate(qcs):
                    bA = 3 if qi == 0 else 5
                    bB = 4 if qi == 0 else 6
                    rq = 0
                    if qi == 1:
                        kb.wait("vector", "V", yv)
                    kb.emit("vector", lambda v, bA=bA, rq=rq: v.tensor_scalar(
                        recip_s[:, rq:rq + 7], ap3(ps[bA], 64, [[66, 7]]), EPS_ATTN, None, ALU.add))
                    kb.emit("vector", lambda v, bB=bB, rq=rq: v.tensor_scalar(
                        recip_s[:, rq + 7:rq + 14], ap3(ps[bB], 64, [[66, 7]]), EPS_ATTN, None, ALU.add))
                    tdn = kb.emit("vector", lambda v, qi=qi, rq=rq: v.tensor_scalar(
                        recip_s[:, rq + 14:rq + 16], ap3(ps[7], qi * 132 + 64, [[66, 2]]), EPS_ATTN, None, ALU.add),
                        inc="V")
                    kb.wait("vector", "V", tdn)
                    trc = kb.emit("vector", lambda v, rq=rq: v.reciprocal(
                        recip_s[:, rq:rq + 16], recip_s[:, rq:rq + 16]), inc="V")
                    kb.wait("vector", "V", trc)
                    trip = ((ap3(y_b, qc * D, [[64, 7], [1, 64]]),
                             ap3(ps[bA], 0, [[66, 7], [1, 64]]),
                             ap3(recip_s, rq, [[1, 7], [0, 64]])),
                            (ap3(y_b, qc * D + 7 * 64, [[64, 7], [1, 64]]),
                             ap3(ps[bB], 0, [[66, 7], [1, 64]]),
                             ap3(recip_s, rq + 7, [[1, 7], [0, 64]])),
                            (ap3(y_b, qc * D + 14 * 64, [[64, 2], [1, 64]]),
                             ap3(ps[7], qi * 132, [[66, 2], [1, 64]]),
                             ap3(recip_s, rq + 14, [[1, 2], [0, 64]])))
                    for o_y, i_n, r_r in trip:
                        yv = kb.emit("vector", lambda v, o=o_y, i=i_n, r=r_r:
                                     v.scalar_tensor_tensor(o, i, 1.0, r, ALU.mult, ALU.mult),
                                     inc="V")
                    if bc == 4 and pair == 0 and qi == 0 and "rcp_q0" in taps:
                        tap_now("rcp_q0", recip_s[:, 0:16], (128, 16), "V")
                kb.wait("tensor", "V", yv)
                if bc == 4 and pair == 0 and "y_p0" in taps:
                    tap_now("y_p0", ap3(y_b, 0, [[1, 2048]]), (128, 2048), "V")
            if bc == 4:
                tap_now("y_b4", ap3(y_b, 0, [[1, 4096]]), (128, 4096), "V")

        kb.mark(f"scan{bc}")
        kb.wait("tensor", "V", v_after_gen)
        kb.wait("tensor", "G", g_after_gen)
        wave_id = 0
        sv3_hist = []
        for fc in range(4):
            for h0, hn in ((0, 7), (7, 7), (14, 2)):
                bank = 3 if (wave_id % 2 == 0) else 4
                wave_id += 1
                if len(sv3_hist) >= 2:
                    kb.wait("tensor", "V", sv3_hist[-2])
                tds = None
                for sub in range(NSUB):
                    for j in range(hn):
                        h = h0 + j
                        if fc < 2:
                            lhs = ap3(kft_b, (sub * H + h) * DE + fc * 128, [[1, 128]])
                            o = ps[bank][:, j * 66:j * 66 + 65]
                        elif fc == 2:
                            lhs = ap3(kft_b, (sub * H + h) * DE + 256, [[1, 16]])
                            o = ps[bank][0:16, j * 66:j * 66 + 65]
                        else:
                            lhs = ap3(kft_b, (sub * H + h) * DE + 272, [[1, 1]])
                            o = ps[bank][0:1, j * 66:j * 66 + 65]
                        r_v = ap3(vaug_b, sub * H * 65 + h * 65, [[1, 65]])
                        st = (sub == 0 and j == 0)
                        sp = (sub == NSUB - 1 and j == hn - 1)
                        tds = kb.emit("tensor", lambda t, o=o, l=lhs, r=r_v, st=st, sp=sp:
                                      MM(t, o, l, r, st, sp), inc=("T" if sp else None))
                kb.wait("vector", "T", tds)
                if fc < 2:
                    dst = s0f if fc == 0 else s1f
                    i_p = ap3(ps[bank], 0, [[66, hn], [1, 65]])
                    o_s3 = ap3(dst, h0 * 65, [[65, hn], [1, 65]])
                elif fc == 2:
                    dst = s2f
                    i_p = AP(ps[bank].tensor, ps[bank].offset, [[ps[bank].ap[0][0], 16], [66, hn], [1, 65]])
                    o_s3 = AP(s2f.tensor, s2f.offset + h0 * 65, [[s2f.ap[0][0], 16], [65, hn], [1, 65]])
                else:
                    i_p = AP(ps[bank].tensor, ps[bank].offset, [[ps[bank].ap[0][0], 1], [66, hn], [1, 65]])
                    o_s3 = AP(scf.tensor, scf.offset + h0 * 65, [[scf.ap[0][0], 1], [65, hn], [1, 65]])
                sv3 = kb.emit("vector", lambda v, o=o_s3, i=i_p: v.tensor_tensor(
                    o, o, i, ALU.add), inc="V")
                sv3_hist.append(sv3)
        kb.wait("tensor", "V", sv3_hist[-1])
        t_after_ds = kb.now("T")

        kb.mark(f"state{bc}")
        if own:
            for sub in range(NSUB):
                co = (bc - NBIG // 2) * NSUB + sub
                xr = sub % 2
                if co - 2 in v_after_h:
                    kb.wait("sync", "V", v_after_h[co - 2])
                xts = "XT0" if co % 2 == 0 else "XT1"
                dxt = kb.dma("sync", xtok_b[:, xr * D:(xr + 1) * D],
                             dram["x_tok"][co * 128:(co + 1) * 128, :], xts)
                for j4 in range(2):
                    for j in range(4):
                        jj = j4 * 4 + j
                        i_y = ap3(y_b, sub * D + jj * 128, [[1, 128]])
                        tv4 = kb.emit("tensor", lambda t, i=i_y, j=j: t.transpose(
                            psb[3][:, j * 128:(j + 1) * 128], i, ident_b[:]),
                            inc=("T" if j == 3 else None))
                    kb.wait("scalar", "T", tv4)
                    o_yt = ap3(yT_b, j4 * 512, [[1, 512]])
                    sv4 = kb.emit("scalar", lambda s_, o=o_yt: s_.copy(o, psb[3][:, 0:512]), inc="S")
                    kb.wait("tensor", "S", sv4)
                for ic in range(8):
                    st, sp = (ic == 0), (ic == 7)
                    l_y = ap3(yT_b, ic * 128, [[1, 128]])
                    kb.emit("tensor", lambda t, l=l_y, ic=ic, st=st, sp=sp: MM(
                        t, ps[5][:], l, ap3(wo_b, ic * D, [[1, 512]]), st, sp))
                    two = kb.emit("tensor", lambda t, l=l_y, ic=ic, st=st, sp=sp: MM(
                        t, ps[6][:], l, ap3(wo_b, ic * D + 512, [[1, 512]]), st, sp),
                        inc=("T" if sp else None))
                kb.wait("vector", "T", two)
                kb.wait("vector", xts, dxt)
                if co - 2 in ds_after_x1:
                    kb.wait("vector", "X1W0" if (co - 2) % 2 == 0 else "X1W1", ds_after_x1[co - 2])
                hr = sub % 2
                o_h0 = h_b[:, hr * D: hr * D + 512]
                o_h1 = h_b[:, hr * D + 512: (hr + 1) * D]
                kb.emit("vector", lambda v, o=o_h0, xr=xr: v.tensor_tensor(
                    o, xtok_b[:, xr * D: xr * D + 512], ps[5][:], ALU.add))
                hv = kb.emit("vector", lambda v, o=o_h1, xr=xr: v.tensor_tensor(
                    o, xtok_b[:, xr * D + 512:(xr + 1) * D], ps[6][:], ALU.add), inc="V")
                v_after_h[co] = hv
                kb.wait("tensor", "V", hv)
                if co == 0:
                    tap_now("h_c0", h_b[:, 0:1024], (128, 1024), "V")
                vap = ln_tile(h_b[:, hr * D:(hr + 1) * D], 0,
                              h_b[:, hr * D:(hr + 1) * D], None)
                kb.wait("sync", "V", vap)
                ds_after_x1[co] = kb.dma("sync", x1_scr[co * 128:(co + 1) * 128, :],
                                         h_b[:, hr * D:(hr + 1) * D],
                                         "X1W0" if co % 2 == 0 else "X1W1")

    kb.mark("attn")
    kb.wait("sync", "X1W0", kb.now("X1W0"))
    kb.wait("sync", "X1W1", kb.now("X1W1"))
    t_x1tr = {}
    vx1_by_co = {}
    for co in range(16):
        xr = co % 2
        x1s = "X1R0" if co % 2 == 0 else "X1R1"
        if co - 2 in vx1_by_co:
            kb.wait("sync", "V", vx1_by_co[co - 2])
        dx1 = kb.dma("sync", xtok_b[:, xr * D:(xr + 1) * D],
                     x1_scr[co * 128:(co + 1) * 128, :], x1s)
        kb.wait("vector", x1s, dx1)
        if co - 2 in t_x1tr:
            kb.wait("vector", "T", t_x1tr[co - 2])
        vx1 = kb.emit("vector", lambda v, xr=xr: v.tensor_copy(
            x1bf_b[:, xr * D:(xr + 1) * D], xtok_b[:, xr * D:(xr + 1) * D]), inc="V")
        vx1_by_co[co] = vx1
        kb.wait("tensor", "V", vx1)
        for j4 in range(2):
            for j in range(4):
                jj = j4 * 4 + j
                i_x = x1bf_b[:, xr * D + jj * 128: xr * D + (jj + 1) * 128]
                tvx = kb.emit("tensor", lambda t, i=i_x, j=j: t.transpose(
                    psb[3][:, j * 128:(j + 1) * 128], i, ident_b[:]),
                    inc=("T" if j == 3 else None))
            t_x1tr[co] = tvx
            kb.wait("scalar", "T", tvx)
            o_fm = ap3(x1fm_b, (j4 * 4) * OWN + co * 128, [[OWN, 4], [1, 128]])
            svx = kb.emit("scalar", lambda s_, o=o_fm: s_.copy(
                o, psb[3][:, 0:512].rearrange("p (g t) -> p g t", g=4)), inc="S")
            kb.wait("tensor", "S", svx)

    add_tap("x1fm", ap3(x1fm_b, 0, [[1, 2048]]), (128, 2048))

    kb.mark("x1fm")
    t_after_moe_e = {}
    t_after_swg = {}
    t_after_sw2 = {}
    v_after_out = {}
    do_after_out = {}
    v_x1bf_last = kb.now("V")
    for oc in range(8):
        o_i = ap3(omoe_b, oc * 2048, [[8, 256], [1, 8]])
        i_i = ap3(b2p_s, oc * 8, [[0, 256], [1, 8]])
        kb.emit("vector", lambda v, o=o_i, i=i_i: v.tensor_copy(o, i))
    vini = kb.emit("vector", lambda v: v.engine_nop(), inc="V")
    kb.wait("tensor", "V", vini)
    tap_now("om_init", ap3(omoe_b, 0, [[1, 1024]]), (128, 1024), "V")
    sh_hist = []
    vw2_hist = []
    for e in range(NE):
        if e - 1 in t_after_moe_e:
            kb.wait("gpsimd", "T", t_after_moe_e[e - 1])
        else:
            kb.wait("gpsimd", "T", kb.now("T"))
        dw1 = kb.dma("gpsimd", ap3(w1_b, 0, [[D, 8], [1, D]]),
                     dram["w1"][e].rearrange("(c p) n -> p c n", p=128), "WM1")
        dw2 = kb.dma("gpsimd", ap3(w2_b, 0, [[D, 8], [1, D]]),
                     dram["w2"][e].rearrange("(c p) n -> p c n", p=128), "WM2")
        lo, delta = MOE_RES[e]
        kb.wait("tensor", "WM1", dw1)
        kb.wait("scalar", "T", kb.now("T"))
        for hf in range(8):
            bank = hf % 2
            if len(sh_hist) >= 2:
                kb.wait("tensor", "S", sh_hist[-2])
            for ic in range(8):
                rhs = ap3(x1fm_b, ic * OWN + lo, [[8, 256], [delta, 2]])
                l_w = ap3(w1_b, ic * D + hf * 128, [[1, 128]])
                o_h = ps[bank][:, 0:512].rearrange("p (g r) -> p g r", g=256)
                th = kb.emit("tensor", lambda t, o=o_h, l=l_w, r=rhs, ic=ic:
                             MM(t, o, l, r, ic == 0, ic == 7), inc=("T" if ic == 7 else None))
            kb.wait("scalar", "T", th)
            sh = kb.emit("scalar", lambda s_, e=e, hf=hf, bank=bank: s_.activation(
                hmoe_b[:, hf * 512:(hf + 1) * 512], ps[bank][:, 0:512], AF.Gelu,
                bias=b1_s[:, e * 8 + hf: e * 8 + hf + 1], scale=1.0), inc="S")
            sh_hist.append(sh)
        if e == 0:
            kb.wait("gpsimd", "S", sh_hist[-1])
            tap_now("hm_e0", hmoe_b[:, 0:2048], (128, 2048), "S")
        kb.wait("tensor", "WM2", dw2)
        kb.wait("tensor", "S", sh_hist[-1])
        for oc in range(8):
            bank = 2 + oc % 2
            if len(vw2_hist) >= 2:
                kb.wait("tensor", "V", vw2_hist[-2])
            for hf in range(8):
                l_w = ap3(w2_b, hf * D + oc * 128, [[1, 128]])
                tw2 = kb.emit("tensor", lambda t, l=l_w, hf=hf, bank=bank: MM(
                    t, ps[bank][:, 0:512], l, hmoe_b[:, hf * 512:(hf + 1) * 512],
                    hf == 0, hf == 7), inc=("T" if hf == 7 else None))
            kb.wait("vector", "T", tw2)
            o_m = ap3(omoe_b, oc * 2048 + lo, [[8, 256], [delta, 2]])
            i_m = ps[bank][:, 0:512].rearrange("p (g r) -> p g r", g=256)
            vw2 = kb.emit("vector", lambda v, o=o_m, i=i_m: v.scalar_tensor_tensor(
                o, i, 0.25, o, ALU.mult, ALU.add), inc="V")
            vw2_hist.append(vw2)
        t_after_moe_e[e] = kb.now("T")
    tap_now("om_moe", ap3(omoe_b, 0, [[1, 1024]]), (128, 1024), "V")

    kb.mark("moe")
    kb.wait("tensor", "S", sh_hist[-1])
    kb.wait("tensor", "V", vw2_hist[-1])
    tsg_hist = []
    tf_hist = []
    vtf_hist = []
    sq2_b = bfv("sq2", 512)
    for half in range(2):
        tok0 = half * 1024
        for g in range(4):
            gr = g % 2
            key = (half, g - 2)
            if key in t_after_swg:
                kb.wait("gpsimd", "T", t_after_swg[key])
            elif g < 2:
                kb.wait("gpsimd", "T",
                        t_after_moe_e[NE - 1] if half == 0 else t_after_swg[(0, 3)])
            sws = "SWE" if g % 2 == 0 else "SWO"
            dsw1 = kb.dma("gpsimd", ap3(sw1_b, gr * 4096, [[512, 8], [1, 512]]),
                          dram["sw1"][:, g * 512:(g + 1) * 512].rearrange("(c p) n -> p c n", p=128), sws)
            dsw3 = kb.dma("gpsimd", ap3(sw3_b, gr * 4096, [[512, 8], [1, 512]]),
                          dram["sw3"][:, g * 512:(g + 1) * 512].rearrange("(c p) n -> p c n", p=128), sws)
            kb.wait("tensor", sws, dsw3)
            for hfl in range(4):
                hf = g * 4 + hfl
                for tt in range(2):
                    bg = tt % 2
                    bu = 2 + tt % 2
                    if len(tsg_hist) >= 2:
                        kb.wait("tensor", "V", tsg_hist[-2])
                        kb.wait("scalar", "V", tsg_hist[-2])
                    for ic in range(8):
                        rhs = ap3(x1fm_b, ic * OWN + tok0 + tt * 512, [[1, 512]])
                        l1 = ap3(sw1_b, gr * 4096 + ic * 512 + hfl * 128, [[1, 128]])
                        l3 = ap3(sw3_b, gr * 4096 + ic * 512 + hfl * 128, [[1, 128]])
                        kb.emit("tensor", lambda t, l=l1, r=rhs, ic=ic, bg=bg:
                                MM(t, ps[bg][:], l, r, ic == 0, ic == 7))
                        tgu = kb.emit("tensor", lambda t, l=l3, r=rhs, ic=ic, bu=bu:
                                      MM(t, ps[bu][:], l, r, ic == 0, ic == 7),
                                      inc=("T" if ic == 7 else None))
                    kb.wait("scalar", "T", tgu)
                    sq_t = sq_b[:, 0:512] if tt == 0 else sq2_b[:, 0:512]
                    du_t = dump_b[:, 0:512] if tt == 0 else dump_b[:, 512:1024]
                    tsi = kb.emit("scalar", lambda s_, hf=hf, bg=bg, o=sq_t: s_.activation(
                        o, ps[bg][:], AF.Silu,
                        bias=sb1_s[:, hf:hf + 1], scale=1.0), inc="S")
                    kb.wait("vector", "S", tsi)
                    kb.emit("vector", lambda v, hf=hf, bu=bu, o=du_t: v.tensor_scalar(
                        o, ps[bu][:], sb3_s[:, hf:hf + 1], 0.5, ALU.add, ALU.mult))
                    o_sg = ap3(sgate_b, hf * 1024 + tt * 512, [[1, 512]])
                    tsg = kb.emit("vector", lambda v, o=o_sg, a=sq_t, b=du_t: v.tensor_tensor(
                        o, a, b, ALU.mult), inc="V")
                    tsg_hist.append(tsg)
                    if half == 0 and hf == 0 and tt == 1:
                        kb.wait("tensor", "V", tsg)
                        tap_now("sg_0", ap3(sgate_b, 0, [[1, 1024]]), (128, 1024), "V")
            t_after_swg[(half, g)] = kb.now("T")
        kb.mark(f"sg{half}")
        kb.wait("tensor", "V", tsg_hist[-1])
        for oc in range(8):
            ocr = oc % 2
            key = (half, oc - 2)
            if key in t_after_sw2:
                kb.wait("gpsimd", "T", t_after_sw2[key])
            elif oc < 2:
                kb.wait("gpsimd", "T",
                        t_after_moe_e[NE - 1] if half == 0 else t_after_sw2[(0, 7)])
            w2sem = "W2E" if oc % 2 == 0 else "W2O"
            dsw2 = kb.dma("gpsimd", ap3(sw2_b, ocr * 2048, [[128, 16], [1, 128]]),
                          dram["sw2"][:, oc * 128:(oc + 1) * 128].rearrange("(c p) n -> p c n", p=128), w2sem)
            kb.wait("tensor", w2sem, dsw2)
            for tt in range(2):
                bank = 4 + tt % 2
                if len(tf_hist) >= 2:
                    kb.wait("tensor", "V", tf_hist[-2])
                for hf in range(16):
                    l_w = ap3(sw2_b, ocr * 2048 + hf * 128, [[1, 128]])
                    r_s = ap3(sgate_b, hf * 1024 + tt * 512, [[1, 512]])
                    tsw2 = kb.emit("tensor", lambda t, l=l_w, r=r_s, hf=hf, bank=bank:
                                   MM(t, ps[bank][:], l, r, hf == 0, hf == 15),
                                   inc=("T" if hf == 15 else None))
                kb.wait("vector", "T", tsw2)
                o_f = ap3(omoe_b, oc * 2048 + tok0 + tt * 512, [[1, 512]])
                tf = kb.emit("vector", lambda v, o=o_f, oc=oc, bank=bank: v.scalar_tensor_tensor(
                    o, ps[bank][:], sb2h_s[:, oc:oc + 1], o, ALU.add, ALU.add), inc="V")
                tf_hist.append(tf)
            if half == 0 and oc == 0:
                tap_now("f_0", ap3(omoe_b, 0, [[1, 1024]]), (128, 1024), "V")
            t_after_sw2[(half, oc)] = kb.now("T")
        kb.mark(f"sw2{half}")
        kb.wait("tensor", "V", tf_hist[-1])
        for ct in range(8):
            co = half * 8 + ct
            xr = ct % 2
            if co - 2 in v_after_out:
                kb.wait("sync", "V", v_after_out[co - 2])
            elif half == 0 and ct < 2:
                kb.wait("sync", "V", v_x1bf_last)
            x1s = "X1R0" if co % 2 == 0 else "X1R1"
            dx1 = kb.dma("sync", xtok_b[:, xr * D:(xr + 1) * D],
                         x1_scr[co * 128:(co + 1) * 128, :], x1s)
            kb.wait("vector", x1s, dx1)
            hr = ct % 2
            for j in range(8):
                bank = 6 + j % 2
                if len(vtf_hist) >= 2:
                    kb.wait("tensor", "V", vtf_hist[-2])
                i_f = ap3(omoe_b, j * 2048 + co * 128, [[1, 128]])
                tfj = kb.emit("tensor", lambda t, i=i_f, bank=bank: t.transpose(
                    ps[bank][:, 0:128], i, ident_f[:]), inc="T")
                kb.wait("vector", "T", tfj)
                o_h = h_b[:, hr * D + j * 128: hr * D + (j + 1) * 128]
                i_x = xtok_b[:, xr * D + j * 128: xr * D + (j + 1) * 128]
                vtf = kb.emit("vector", lambda v, o=o_h, i=i_x, bank=bank: v.tensor_tensor(
                    o, i, ps[bank][:, 0:128], ALU.add), inc="V")
                vtf_hist.append(vtf)
            if co - 2 in do_after_out:
                kb.wait("vector", "DO0" if (co - 2) % 2 == 0 else "DO1", do_after_out[co - 2])
            if half == 0 and ct == 0:
                tap_now("h2_0", h_b[:, 0:1024], (128, 1024), "V")
            vo = ln_tile(h_b[:, hr * D:(hr + 1) * D], 1,
                         x2o_b[:, xr * D:(xr + 1) * D], None)
            v_after_out[co] = vo
            kb.wait("sync", "V", vo)
            do_after_out[co] = kb.dma("sync", out_d[co * 128:(co + 1) * 128, :],
                                      x2o_b[:, xr * D:(xr + 1) * D],
                                      "DO0" if co % 2 == 0 else "DO1")

    for t, src_ap in tap_dmas:
        kb.dma("gpsimd", t, src_ap, "DTAP")
    kb.wait("sync", "DO0", kb.now("DO0"))
    kb.wait("sync", "DO1", kb.now("DO1"))
    kb.wait("sync", "DTAP", kb.now("DTAP"))

    if cut is not None:
        kb.cut_to(cut)
        kb.sem("DCUT")
        for co in range(16):
            kb.dma("sync", out_d[co * 128:(co + 1) * 128, :],
                   x2o_b[:, (co % 2) * D:((co % 2) + 1) * D], "DCUT")
        kb.wait("sync", "DCUT", kb.now("DCUT"))

    with ctx:
        with nc.Block() as block:
            @block.sync
            def _(sync):
                for f in kb.ops["sync"]:
                    f(sync)

            @block.tensor
            def _(tns):
                for f in kb.ops["tensor"]:
                    f(tns)

            @block.vector
            def _(vec):
                for f in kb.ops["vector"]:
                    f(vec)

            @block.scalar
            def _(scl):
                for f in kb.ops["scalar"]:
                    f(scl)

            @block.gpsimd
            def _(gps):
                for f in kb.ops["gpsimd"]:
                    f(gps)
    return nc


def stage_inputs(inputs):
    x = np.ascontiguousarray(np.asarray(inputs["L0_attn_hidden_states"], dtype=np.float32))
    Wq = np.asarray(inputs["L0_attn_Wq"], np.float32)
    Wk = np.asarray(inputs["L0_attn_Wk"], np.float32)
    Wv = np.asarray(inputs["L0_attn_Wv"], np.float32)
    Wo = np.asarray(inputs["L0_attn_Wo"], np.float32)
    wcat = np.ascontiguousarray(np.concatenate([Wq.T, Wk.T, Wv.T], axis=1))
    wo_in = np.ascontiguousarray(Wo.T)
    sw1 = np.ascontiguousarray(np.asarray(inputs["L0_ffn_sw1"], np.float32))
    sw3 = np.ascontiguousarray(np.asarray(inputs["L0_ffn_sw3"], np.float32))
    sw2 = np.ascontiguousarray(np.asarray(inputs["L0_ffn_sw2"], np.float32))
    sb1 = np.ascontiguousarray(np.asarray(inputs["L0_ffn_sb1"], np.float32).reshape(16, 128).T)
    sb3 = np.ascontiguousarray(np.asarray(inputs["L0_ffn_sb3"], np.float32).reshape(16, 128).T)
    sb2 = np.asarray(inputs["L0_ffn_sb2"], np.float32)
    sb2h = np.ascontiguousarray((0.5 * sb2).reshape(8, 128).T)
    w1 = np.ascontiguousarray(np.asarray(inputs["L0_ffn_w1"], np.float32))
    b1 = np.asarray(inputs["L0_ffn_b1"], np.float32)
    b1r = np.ascontiguousarray(b1.reshape(NE, 8, 128).transpose(2, 0, 1))
    w2 = np.ascontiguousarray(np.asarray(inputs["L0_ffn_w2"], np.float32))
    b2 = np.asarray(inputs["L0_ffn_b2"], np.float32)
    b2p = np.zeros((128, 8, 8), np.float32)
    for r in range(8):
        e0 = (3 * r) % 8
        e1 = (3 * r + 7) % 8
        b2p[:, :, r] = (0.25 * (b2[e0] + b2[e1])).reshape(8, 128).T
    maskh = np.ascontiguousarray(np.triu(np.ones((128, 128), np.float32)) * 0.5)
    ident = np.eye(128, dtype=np.float32)
    rep4 = np.zeros((16, 128), np.float32)
    for g in range(4):
        rep4[np.arange(16), 32 * g + np.arange(16)] = 1.0
    sel16 = np.zeros((16, 1024), np.float32)
    for j in range(8):
        sel16[np.arange(16), j * 128 + 16 * j + np.arange(16)] = 1.0

    per_core = []
    for b in range(4):
        for half in range(2):
            if half == 0:
                win = np.concatenate([np.zeros((2048, D), np.float32), x[b, :2048]], axis=0)
                val = np.concatenate([np.zeros(16, np.float32), np.ones(16, np.float32)])
            else:
                win = x[b]
                val = np.ones(32, np.float32)
            per_core.append({
                "x_fm": np.ascontiguousarray(win.T),
                "x_tok": np.ascontiguousarray(x[b, half * 2048:(half + 1) * 2048]),
                "valid": np.ascontiguousarray(np.broadcast_to(val[None, :], (128, NCH))),
                "wcat": wcat, "wo_in": wo_in,
                "sw1": sw1, "sw3": sw3, "sw2": sw2,
                "sb1": sb1, "sb3": sb3, "sb2h": sb2h,
                "w1": w1, "b1": b1r, "b2p": b2p, "w2": w2,
                "maskh": maskh, "ident": ident, "rep4": rep4, "sel16": sel16,
            })
    return per_core


def kernel(**inputs):
    per_core = stage_inputs(inputs)
    nc = build_program()
    res = run_bass_kernel_spmd(nc, per_core, core_ids=list(range(8)))
    out = np.empty((4, 4096, 1024), np.float32)
    for b in range(4):
        for half in range(2):
            out[b, half * 2048:(half + 1) * 2048] = res.results[b * 2 + half]["out"]
    return out


# version 08
# speedup vs baseline: 1.2454x; 1.2454x over previous
import numpy as np
import concourse.bass as bass
import concourse.mybir as mybir
from concourse.ap import AP
from concourse.bass_utils import run_bass_kernel_spmd

F32 = mybir.dt.float32
BF16 = mybir.dt.bfloat16
ALU = mybir.AluOpType
AF = mybir.ActivationFunctionType
AXX = mybir.AxisListType.X

D = 1024
H = 16
FT = 16
WIN = 4096
OWN = 2048
NCH = 32
NBIG = 8
NSUB = 4
DE = 273
HS = 2048
NE = 8
EPS_ATTN = 1e-12
EPS_LN = 1e-5

MOE_RES = []
for e in range(NE):
    r0 = (3 * e) % 8
    r1 = (3 * e + 3) % 8
    MOE_RES.append((min(r0, r1), abs(r1 - r0)))


class KB:
    ENGINES = ("sync", "tensor", "vector", "scalar", "gpsimd")

    def __init__(self, nc):
        self.nc = nc
        self.ops = {e: [] for e in self.ENGINES}
        self.sems = {}
        self.cnt = {}

    def sem(self, name):
        self.sems[name] = self.nc.alloc_semaphore(name)
        self.cnt[name] = 0

    def emit(self, eng, fn, inc=None):
        if inc is not None:
            self.cnt[inc] += 1
            sem = self.sems[inc]
            self.ops[eng].append(lambda e, fn=fn, sem=sem: fn(e).then_inc(sem, 1))
            return self.cnt[inc]
        self.ops[eng].append(fn)
        return None

    def wait(self, eng, sem_name, val):
        if val is None or val <= 0:
            return
        sem = self.sems[sem_name]
        self.ops[eng].append(lambda e, sem=sem, v=val: e.wait_ge(sem, v))

    def now(self, s):
        return self.cnt[s]

    def mark(self, name):
        self.marks = getattr(self, "marks", {})
        self.marks[name] = {e: len(self.ops[e]) for e in self.ENGINES}

    def cut_to(self, name):
        m = self.marks[name]
        for e in self.ENGINES:
            self.ops[e] = self.ops[e][:m[e]]

    def dma(self, eng, out, in_, sem_name):
        self.cnt[sem_name] += 16
        sem = self.sems[sem_name]
        self.ops[eng].append(
            lambda e, out=out, in_=in_, sem=sem: e.dma_start(out=out, in_=in_).then_inc(sem, 16))
        return self.cnt[sem_name]


def ap3(t_ap, base_off, dims):
    return AP(t_ap.tensor, t_ap.offset + base_off,
              [list(t_ap.ap[0])] + [list(d) for d in dims])


def build_program(taps=(), cut=None, no_inter=False, races=False):
    nc = bass.Bass("TRN2", target_bir_lowering=False, detect_race_conditions=races)
    dram = {}

    def din(name, shape):
        dram[name] = nc.dram_tensor(name, list(shape), F32, kind="ExternalInput").ap()
        return dram[name]

    din("x_fm", (D, WIN))
    din("x_tok", (OWN, D))
    din("valid", (128, NCH))
    din("wcat", (D, 1536))
    din("wo_in", (D, D))
    din("sw1", (D, HS))
    din("sw3", (D, HS))
    din("sw2", (HS, D))
    din("sb1", (128, 16))
    din("sb3", (128, 16))
    din("sb2h", (128, 8))
    din("w1", (NE, D, D))
    din("b1", (128, NE, 8))
    din("b2p", (128, 8, 8))
    din("w2", (NE, D, D))
    din("maskh", (128, 128))
    din("ident", (128, 128))
    din("rep4", (16, 128))
    din("sel16", (16, 1024))
    out_d = nc.dram_tensor("out", [OWN, D], F32, kind="ExternalOutput").ap()
    x1_scr = nc.dram_tensor("x1_scr", [OWN, D], F32, kind="Internal").ap()

    from contextlib import ExitStack
    ctx = ExitStack()

    def sb(name, shape, dt=BF16):
        return ctx.enter_context(nc.sbuf_tensor(name, list(shape), dt)).ap()

    maskh_b = sb("maskh_b", (128, 128))
    ident_b = sb("ident_b", (128, 128))
    ident_f = sb("ident_f", (128, 128), F32)
    rep4_b = sb("rep4_b", (16, 128))
    sel16_b = sb("sel16_b", (16, 1024))
    ones_row = sb("ones_row", (1, 128))
    valid_s = sb("valid_s", (128, NCH), F32)
    sb1_s = sb("sb1_s", (128, 16), F32)
    sb3_s = sb("sb3_s", (128, 16), F32)
    sb2h_s = sb("sb2h_s", (128, 8), F32)
    b1_s = sb("b1_s", (128, NE * 8), F32)
    b2p_s = sb("b2p_s", (128, NE * 8), F32)
    xtok_b = sb("xtok_b", (128, 2 * D), F32)
    h_b = sb("h_b", (128, 2 * D), F32)
    dump_b = sb("dump_b", (128, D), F32)
    x1bf_b = sb("x1bf_b", (128, 2 * D))
    x2o_b = sb("x2o_b", (128, 2 * D), F32)
    m_b = sb("m_b", (128, 2), F32)
    mneg_b = sb("mneg_b", (128, 2), F32)
    var_b = sb("var_b", (128, 2), F32)
    rstd_b = sb("rstd_b", (128, 2), F32)
    recip_s = sb("recip_s", (128, 16), F32)
    eps_b = sb("eps_b", (128, 1), F32)

    AR_BYTES = 175424
    AR = sb("arena", (128, AR_BYTES // 4), F32)
    ARB = AR.bitcast(BF16)

    off = {}

    def lay(name, nbytes, at=None):
        if at is None:
            at = lay.cur
            lay.cur += (nbytes + 63) // 64 * 64
        off[name] = at
        assert at + nbytes <= AR_BYTES, (name, at, nbytes)
        return at
    lay.cur = 0

    def bfv(name, cols, p0=0, np_=128):
        o = off[name]
        assert o % 2 == 0
        a = ARB[:, o // 2: o // 2 + cols]
        if p0 or np_ != 128:
            a = AP(a.tensor, a.offset + p0 * a.ap[0][0], [[a.ap[0][0], np_]] + [list(d) for d in a.ap[1:]])
        return a

    def f32v(name, cols, p0=0, np_=128):
        o = off[name]
        assert o % 4 == 0
        a = AR[:, o // 4: o // 4 + cols]
        if p0 or np_ != 128:
            a = AP(a.tensor, a.offset + p0 * a.ap[0][0], [[a.ap[0][0], np_]] + [list(d) for d in a.ap[1:]])
        return a

    lay("wcat", 8 * 1536 * 2)
    lay("wo", 8 * D * 2)
    kft_at = lay("kft", NSUB * H * DE * 2)
    lay("qfx2", NSUB * H * 256 * 2)
    lay("vaug", NSUB * H * 65 * 2)
    lay("qT", NSUB * 256 * 2)
    lay("kT", NSUB * 256 * 2)
    lay("qfm", 2 * 512 * 2)
    lay("qfmS", 2 * 512 * 2)
    lay("kfm", 2 * 512 * 2)
    lay("kfmS", 2 * 512 * 2)
    lay("yqfT", max(NSUB * D, 4096) * 2)
    lay("yT", 8 * 128 * 2)
    lay("sq", 512 * 2)
    lay("atm", 1024 * 2)
    lay("s0f", H * 65 * 4)
    lay("s1f", H * 65 * 4)
    lay("s2f", H * 65 * 4)
    lay("scf", H * 65 * 4)
    lay("s0b", H * 65 * 2)
    lay("s1b", H * 65 * 2)
    lay("s2b", H * 65 * 2)
    lay("s2b4", H * 65 * 2)
    lay("scb", H * 65 * 2)
    lay("xfm", 2 * 1024 * 2)
    a_end = lay.cur
    assert a_end <= AR_BYTES, a_end

    lay("x1fm", 8 * OWN * 2, at=kft_at)
    lay("w1s", 8 * D * 2, at=0)
    lay("w2s", 8 * D * 2, at=8 * D * 2)
    lay("sw1s", 2 * 8 * 512 * 2, at=0)
    lay("sw3s", 2 * 8 * 512 * 2, at=8 * D * 2)
    lay("hmoe", 8 * 512 * 2, at=16 * D * 2)
    lay("sw2s", 2 * 16 * 128 * 2, at=16 * D * 2)
    f_cur = kft_at + 8 * OWN * 2
    lay("omoe", 8 * 2048 * 4, at=f_cur); f_cur += 8 * 2048 * 4
    f_cur = max(f_cur, 140544)
    lay("sgate", 16 * 1024 * 2, at=f_cur); f_cur += 16 * 1024 * 2
    lay("sq2", 512 * 2, at=f_cur); f_cur += 512 * 2
    assert f_cur <= AR_BYTES, f_cur

    wcat_b = bfv("wcat", 8 * 1536)
    wo_b = bfv("wo", 8 * D)
    kft_b = bfv("kft", NSUB * H * DE)
    qfx2_b = bfv("qfx2", NSUB * H * 256)
    vaug_b = bfv("vaug", NSUB * H * 65)
    qT_b = bfv("qT", NSUB * 256)
    kT_b = bfv("kT", NSUB * 256)
    qfm_b = bfv("qfm", 2 * 512)
    qfmS_b = bfv("qfmS", 2 * 512)
    kfm_b = bfv("kfm", 2 * 512)
    kfmS_b = bfv("kfmS", 2 * 512)
    y_b = bfv("yqfT", NSUB * D)
    qfT_b = bfv("yqfT", 4096)
    yT_b = bfv("yT", 8 * 128)
    sq_b = bfv("sq", 512)
    atm_b = bfv("atm", 1024)
    s0f = f32v("s0f", H * 65)
    s1f = f32v("s1f", H * 65)
    s2f = f32v("s2f", H * 65, 0, 16)
    scf = f32v("scf", H * 65, 0, 1)
    s0b = bfv("s0b", H * 65)
    s1b = bfv("s1b", H * 65)
    s2b = bfv("s2b", H * 65, 0, 16)
    s2stk = bfv("s2b4", H * 65)
    scb = bfv("scb", H * 65, 0, 1)
    xfm_b = bfv("xfm", 2 * 1024)
    x1fm_b = bfv("x1fm", 8 * OWN)
    w1_b = bfv("w1s", 8 * D)
    w2_b = bfv("w2s", 8 * D)
    hmoe_b = bfv("hmoe", 8 * 512)
    omoe_b = f32v("omoe", 8 * 2048)
    sgate_b = bfv("sgate", 16 * 1024)
    sw1_b = bfv("sw1s", 2 * 8 * 512)
    sw3_b = bfv("sw3s", 2 * 8 * 512)
    sw2_b = bfv("sw2s", 2 * 16 * 128)

    ps = [ctx.enter_context(nc.psum_tensor(f"pb{i}", [128, 512], F32)).ap() for i in range(8)]
    psb = [p.bitcast(BF16) for p in ps]

    kb = KB(nc)
    for s in ("T", "V", "S", "G", "DPX", "DPC", "XF0", "XF1", "WM1", "WM2",
              "SWE", "SWO", "W2E", "W2O", "XT0", "XT1", "X1W0", "X1W1",
              "X1R0", "X1R1", "SH", "DO0", "DO1", "DTAP"):
        kb.sem(s)

    kb.mark("zero")
    tap_dmas = []

    def add_tap(name, src_ap, shape):
        if name in taps:
            t = nc.dram_tensor("tap_" + name, list(shape), F32, kind="ExternalOutput").ap()
            tap_dmas.append((t, src_ap))

    tap_fired = set()

    def tap_now(name, src_ap, shape, wait_eng_sem=None):
        if name not in taps or name in tap_fired:
            return
        tap_fired.add(name)
        t = nc.dram_tensor("tap_" + name, list(shape), F32, kind="ExternalOutput").ap()
        if wait_eng_sem is not None:
            kb.wait("gpsimd", wait_eng_sem, kb.now(wait_eng_sem))
        dv = kb.dma("gpsimd", t, src_ap, "DTAP")
        for eng in ("vector", "tensor", "scalar"):
            kb.wait(eng, "DTAP", dv)

    def MM(t, o, l, r, st, sp, tp=None):
        return t.matmul(o, l, r, start=st, stop=sp, tile_position=tp, skip_group_check=True)

    kb.dma("gpsimd", maskh_b[:], dram["maskh"], "DPC")
    kb.dma("gpsimd", ident_b[:], dram["ident"], "DPC")
    kb.dma("gpsimd", rep4_b[:], dram["rep4"], "DPC")
    kb.dma("gpsimd", sel16_b[:], dram["sel16"], "DPC")
    kb.dma("gpsimd", wcat_b.rearrange("p (c n) -> p c n", c=8),
           dram["wcat"].rearrange("(c p) n -> p c n", p=128), "DPC")
    kb.dma("gpsimd", wo_b.rearrange("p (c n) -> p c n", c=8),
           dram["wo_in"].rearrange("(c p) n -> p c n", p=128), "DPC")
    kb.dma("sync", ident_f[:], dram["ident"], "DPX")
    kb.dma("sync", valid_s[:], dram["valid"], "DPX")
    kb.dma("sync", sb1_s[:], dram["sb1"], "DPX")
    kb.dma("sync", sb3_s[:], dram["sb3"], "DPX")
    kb.dma("sync", sb2h_s[:], dram["sb2h"], "DPX")
    kb.dma("sync", b1_s[:].rearrange("p (e c) -> p e c", e=NE), dram["b1"], "DPX")
    kb.dma("sync", b2p_s[:].rearrange("p (c r) -> p c r", c=8), dram["b2p"], "DPX")
    pre_dx, pre_dc = kb.now("DPX"), kb.now("DPC")
    kb.mark("pre_dma")
    kb.wait("vector", "DPX", pre_dx)
    kb.wait("vector", "DPC", pre_dc)
    kb.emit("vector", lambda v: v.memset(ones_row[:], 1.0))
    kb.emit("vector", lambda v: v.memset(eps_b[:], EPS_LN))
    kb.emit("vector", lambda v: v.memset(s0f, 0.0))
    kb.emit("vector", lambda v: v.memset(s1f, 0.0))
    kb.emit("vector", lambda v: v.memset(s2f, 0.0))
    v_pre = kb.emit("vector", lambda v: v.memset(scf, 0.0), inc="V")
    kb.mark("pre_ms")
    kb.wait("tensor", "V", v_pre)
    kb.wait("scalar", "V", v_pre)
    kb.wait("tensor", "DPC", pre_dc)
    kb.wait("tensor", "DPX", pre_dx)
    kb.wait("gpsimd", "V", v_pre)

    kb.mark("pre")
    t_after_proj = {}
    v_after_evac = {}
    ds_after_x1 = {}
    g_after_gen = 0
    v_after_gen = 0
    t_after_ds = 0
    v_after_h = {}

    def ln_tile(src_f32, slot, out_ap, also_bf=None):
        vr = kb.emit("vector", lambda v: v.reduce_sum(m_b[:, slot:slot + 1], src_f32, AXX), inc="V")
        kb.wait("vector", "V", vr)
        vm = kb.emit("vector", lambda v: v.tensor_scalar(
            mneg_b[:, slot:slot + 1], m_b[:, slot:slot + 1], -1.0 / D, None, ALU.mult), inc="V")
        kb.wait("scalar", "V", vm)
        sv = kb.emit("scalar", lambda s_: s_.activation(
            dump_b[:], src_f32, AF.Square, bias=mneg_b[:, slot:slot + 1], scale=1.0), inc="S")
        kb.wait("vector", "S", sv)
        vv2 = kb.emit("vector", lambda v: v.reduce_sum(var_b[:, slot:slot + 1], dump_b[:], AXX), inc="V")
        kb.wait("scalar", "V", vv2)
        sn = kb.emit("scalar", lambda s_: s_.activation(
            rstd_b[:, slot:slot + 1], var_b[:, slot:slot + 1], AF.Ln, bias=eps_b[:], scale=1.0 / D), inc="S")
        kb.wait("scalar", "S", sn)
        sl = kb.emit("scalar", lambda s_: s_.activation(
            rstd_b[:, slot:slot + 1], rstd_b[:, slot:slot + 1], AF.Exp, bias=0.0, scale=-0.5), inc="S")
        kb.wait("vector", "S", sl)
        vap = kb.emit("vector", lambda v: v.tensor_scalar(
            out_ap, src_f32, mneg_b[:, slot:slot + 1], rstd_b[:, slot:slot + 1],
            ALU.add, ALU.mult), inc="V")
        if also_bf is not None:
            vap = kb.emit("vector", lambda v: v.tensor_copy(also_bf, out_ap), inc="V")
        return vap

    tsq_hist = []
    tat_hist = []
    sv2_hist = []
    srep_hist = []
    for bc in range(NBIG):
        own = bc >= NBIG // 2

        for sub in range(NSUB):
            c = bc * NSUB + sub
            xr = c % 2
            if c - 2 in t_after_proj:
                kb.wait("gpsimd", "T", t_after_proj[c - 2])
            src = dram["x_fm"][:, c * 128:(c + 1) * 128].rearrange("(cc p) t -> p cc t", p=128)
            xfs = "XF0" if c % 2 == 0 else "XF1"
            dxv = kb.dma("gpsimd",
                         ap3(xfm_b, xr * 1024, [[128, 8], [1, 128]]),
                         src, xfs)
            kb.wait("tensor", xfs, dxv)
            if sub == 0:
                kb.wait("vector", "G", kb.now("G"))
            for ic in range(8):
                lhs = ap3(xfm_b, xr * 1024 + ic * 128, [[1, 128]])
                w0 = ap3(wcat_b, ic * 1536, [[1, 512]])
                w1c = ap3(wcat_b, ic * 1536 + 512, [[1, 512]])
                w2c = ap3(wcat_b, ic * 1536 + 1024, [[1, 512]])
                st, sp = (ic == 0), (ic == 7)
                kb.emit("tensor", lambda t, l=lhs, r=w0, st=st, sp=sp: MM(t, ps[0][:], l, r, st, sp))
                kb.emit("tensor", lambda t, l=lhs, r=w1c, st=st, sp=sp: MM(t, ps[1][:], l, r, st, sp))
                tv = kb.emit("tensor", lambda t, l=lhs, r=w2c, st=st, sp=sp: MM(t, ps[2][:], l, r, st, sp),
                             inc=("T" if sp else None))
            t_after_proj[c] = tv
            kb.wait("vector", "T", tv)
            o_qt = ap3(qT_b, sub * 256, [[1, 256]])
            o_kt = ap3(kT_b, sub * 256, [[1, 256]])
            kb.emit("vector", lambda v, o=o_qt: v.tensor_scalar(o, ps[0][:, 0:256], 0.5, None, ALU.mult))
            kb.emit("vector", lambda v, o=o_kt: v.tensor_scalar(o, ps[0][:, 256:512], 0.35355339059327373, None, ALU.mult))
            o_v0 = ap3(vaug_b, sub * H * 65, [[65, 8], [1, 64]])
            o_v1 = ap3(vaug_b, sub * H * 65 + 8 * 65, [[65, 8], [1, 64]])
            kb.emit("vector", lambda v, o=o_v0: v.tensor_copy(o, ps[1][:].rearrange("p (h e) -> p h e", h=8)))
            kb.emit("vector", lambda v, o=o_v1: v.tensor_copy(o, ps[2][:].rearrange("p (h e) -> p h e", h=8)))
            o_vc = ap3(vaug_b, sub * H * 65 + 64, [[65, H]])
            i_vc = ap3(valid_s, c, [[0, H]])
            vv = kb.emit("vector", lambda v, o=o_vc, i=i_vc: v.tensor_copy(o, i), inc="V")
            v_after_evac[c] = vv
            kb.wait("tensor", "V", vv)
            if c == 16:
                tap_now("qt_c16", ap3(qT_b, 0, [[1, 256]]), (128, 256), "V")
                tap_now("vaug_c16", ap3(vaug_b, 0, [[1, 1040]]), (128, 1040), "V")

            if own:
                for ti in range(2):
                    i_q = ap3(qT_b, sub * 256 + ti * 128, [[1, 128]])
                    i_k = ap3(kT_b, sub * 256 + ti * 128, [[1, 128]])
                    kb.emit("tensor", lambda t, i=i_q, ti=ti: t.transpose(
                        psb[3][:, ti * 128:(ti + 1) * 128], i, ident_b[:]))
                    tv2 = kb.emit("tensor", lambda t, i=i_k, ti=ti: t.transpose(
                        psb[3][:, 256 + ti * 128: 256 + (ti + 1) * 128], i, ident_b[:]),
                        inc=("T" if ti == 1 else None))
                kb.wait("scalar", "T", tv2)
                kb.wait("scalar", "SH", kb.now("SH"))
                o_qfm = ap3(qfm_b, sub * 128, [[512, 2], [1, 128]])
                o_kfm = ap3(kfm_b, sub * 128, [[512, 2], [1, 128]])
                kb.emit("scalar", lambda s_, o=o_qfm: s_.copy(
                    o, psb[3][:, 0:256].rearrange("p (ti t) -> p ti t", ti=2)))
                sv = kb.emit("scalar", lambda s_, o=o_kfm: s_.copy(
                    o, psb[3][:, 256:512].rearrange("p (ti t) -> p ti t", ti=2)), inc="S")
                kb.wait("tensor", "S", sv)
                kb.wait("sync", "S", sv)
                for src_t, dst_t in ((qfm_b, qfmS_b), (kfm_b, kfmS_b)):
                    for ti in range(2):
                        i_s = AP(src_t.tensor, src_t.offset + 16 * src_t.ap[0][0] + ti * 512 + sub * 128,
                                 [[src_t.ap[0][0], 112], [1, 128]])
                        o_s = AP(dst_t.tensor, dst_t.offset + ti * 512 + sub * 128,
                                 [[dst_t.ap[0][0], 112], [1, 128]])
                        kb.dma("sync", o_s, i_s, "SH")

        kb.mark(f"proj{bc}")
        kb.wait("vector", "V", v_after_evac[bc * NSUB + NSUB - 1])
        for sub in range(NSUB):
            c = bc * NSUB + sub
            kbase = sub * H * DE
            kt = ap3(kT_b, sub * 256, [[1, 256]])
            for (eng, h0) in (("vector", 0), ("gpsimd", 8)):
                in0 = ap3(kT_b, sub * 256 + h0 * FT, [[FT, 8], [1, FT], [0, FT]])
                in1 = ap3(kT_b, sub * 256 + h0 * FT, [[FT, 8], [0, FT], [1, FT]])
                outa = ap3(kft_b, kbase + h0 * DE, [[DE, 8], [FT, FT], [1, FT]])
                if eng == "gpsimd":
                    kb.wait("gpsimd", "T", t_after_ds)
                    kb.wait("gpsimd", "V", v_after_evac[c])
                kb.emit(eng, lambda e, o=outa, a=in0, b=in1: e.tensor_tensor(
                    o, a, b, ALU.mult))
            o_lin = ap3(kft_b, kbase + 256, [[DE, H], [1, FT]])
            i_lin = ap3(kT_b, sub * 256, [[FT, H], [1, FT]])
            kb.emit("vector", lambda v, o=o_lin, i=i_lin: v.tensor_scalar(
                o, i, 1.4142135623730951, None, ALU.mult))
            o_cst = ap3(kft_b, kbase + 272, [[DE, H]])
            i_cst = ap3(valid_s, c, [[0, H]])
            kb.emit("vector", lambda v, o=o_cst, i=i_cst: v.tensor_copy(o, i))
        v_after_gen = kb.emit("vector", lambda v: v.engine_nop(), inc="V")
        g_after_gen = kb.emit("gpsimd", lambda g: g.engine_nop(), inc="G")
        if bc == 4:
            kb.wait("gpsimd", "V", v_after_gen)
            tap_now("kft_c16", ap3(kft_b, 0, [[1, 4368]]), (128, 4368))

        kb.mark(f"gen{bc}")
        if own:
            for sub in range(NSUB):
                c = bc * NSUB + sub
                kb.wait("vector", "T", kb.now("T"))
                kb.wait("vector", "V", v_after_evac[c])
                for (eng, h0) in (("vector", 0), ("gpsimd", 8)):
                    in0 = ap3(qT_b, sub * 256 + h0 * FT, [[FT, 8], [1, FT], [0, FT]])
                    in1 = ap3(qT_b, sub * 256 + h0 * FT, [[FT, 8], [0, FT], [1, FT]])
                    outa = ap3(qfT_b, h0 * 256, [[256, 8], [FT, FT], [1, FT]])
                    if eng == "gpsimd":
                        kb.wait("gpsimd", "V", v_after_evac[c])
                        kb.wait("gpsimd", "T", kb.now("T"))
                    kb.emit(eng, lambda e, o=outa, a=in0, b=in1: e.tensor_tensor(o, a, b, ALU.mult))
                qv = kb.emit("vector", lambda v: v.engine_nop(), inc="V")
                qg = kb.emit("gpsimd", lambda g: g.engine_nop(), inc="G")
                kb.wait("tensor", "V", qv)
                kb.wait("tensor", "G", qg)
                grp = [(h, cc) for h in range(H) for cc in range(2)]
                for gi in range(0, 32, 4):
                    pbank = psb[4] if (gi // 4) % 2 == 0 else psb[3]
                    if len(sv2_hist) >= 2:
                        kb.wait("tensor", "S", sv2_hist[-2])
                    for j in range(4):
                        h, cc = grp[gi + j]
                        i_t = ap3(qfT_b, h * 256 + cc * 128, [[1, 128]])
                        tv3 = kb.emit("tensor", lambda t, i=i_t, j=j, pb=pbank: t.transpose(
                            pb[:, j * 128:(j + 1) * 128], i, ident_b[:]),
                            inc=("T" if j == 3 else None))
                    kb.wait("scalar", "T", tv3)
                    for j in range(4):
                        h, cc = grp[gi + j]
                        o_q = ap3(qfx2_b, (sub * H + h) * 256 + cc * 128, [[1, 128]])
                        sv2 = kb.emit("scalar", lambda s_, j=j, o=o_q, pb=pbank: s_.copy(
                            o, pb[:, j * 128:(j + 1) * 128]),
                            inc=("S" if j == 3 else None))
                    sv2_hist.append(sv2)
                if sub == 0 and bc == 4:
                    kb.wait("gpsimd", "S", sv2_hist[-1])
                    tap_now("qfx2_c16", ap3(qfx2_b, 0, [[1, 512]]), (128, 512), "S")

            kb.emit("vector", lambda v: v.tensor_copy(s0b, s0f))
            kb.emit("vector", lambda v: v.tensor_copy(s1b, s1f))
            kb.emit("vector", lambda v: v.tensor_copy(s2b, s2f))
            scast = kb.emit("vector", lambda v: v.tensor_copy(scb, scf), inc="V")
            kb.wait("tensor", "V", scast)
            kb.wait("tensor", "SH", kb.now("SH"))
            kb.mark(f"scast{bc}")
            kb.wait("tensor", "S", sv2_hist[-1])
            for ti in range(2):
                for ps_ in range(2):
                    sbank = 2 if (ti * 2 + ps_) % 2 == 0 else 0
                    if len(srep_hist) >= 2:
                        kb.wait("tensor", "S", srep_hist[-2])
                    for j in range(4):
                        jj = ps_ * 4 + j
                        h = ti * 8 + jj
                        l_sel = sel16_b[:, jj * 128:(jj + 1) * 128]
                        r_s2 = ap3(s2b, h * 65, [[1, 65]])
                        trep = kb.emit("tensor", lambda t, l=l_sel, r=r_s2, j=j, sb_=sbank: MM(
                            t, ps[sb_][:, j * 65:(j + 1) * 65], l, r, j == 0, j == 3),
                            inc=("T" if j == 3 else None))
                    kb.wait("scalar", "T", trep)
                    o_st = ap3(s2stk, ti * 520 + ps_ * 260, [[1, 260]])
                    srep = kb.emit("scalar", lambda s_, o=o_st, sb_=sbank: s_.copy(
                        o, ps[sb_][:, 0:260]), inc="S")
                    srep_hist.append(srep)
            kb.wait("tensor", "S", srep_hist[-1])

            kb.mark(f"qfx{bc}")
            kb.mark(f"rep{bc}")
            for pair in range(2):
                qcs = (0, 1) if pair == 0 else (2, 3)
                qhi = (qcs[-1] + 1) * 128

                def num_ap(h, qi):
                    if h < 7:
                        bank = 3 if qi == 0 else 5
                        return ps[bank][:, h * 66:h * 66 + 65]
                    if h < 14:
                        bank = 4 if qi == 0 else 6
                        return ps[bank][:, (h - 7) * 66:(h - 7) * 66 + 65]
                    return ps[7][:, qi * 132 + (h - 14) * 66: qi * 132 + (h - 14) * 66 + 65]

                strip_off = {}
                soff = 0
                for kc in range(qcs[-1] + 1):
                    strip_off[kc] = soff
                    soff += qhi - max(kc, qcs[0]) * 128
                assert soff <= 1024

                for h in range(H):
                    ti, hh = h // 8, h % 8
                    if hh % 2 == 0:
                        base = hh * 16
                        kfm_t, qfm_t = kfm_b, qfm_b
                    else:
                        base = hh * 16 - 16
                        kfm_t, qfm_t = kfmS_b, qfmS_b
                    tp = (96, 0) if base == 96 else None
                    tat = None
                    for kc in range(qcs[-1] + 1):
                        par = len(tsq_hist) % 2
                        sbank = 2 if par == 0 else 0
                        sqo = par * 256
                        if len(tsq_hist) >= 2:
                            kb.wait("tensor", "S", tsq_hist[-2])
                        if len(tat_hist) >= 2:
                            kb.wait("scalar", "V", tat_hist[-2])
                        sp_lo = max(kc, qcs[0]) * 128
                        nq = qhi - sp_lo
                        l_sc = AP(kfm_t.tensor, kfm_t.offset + base * kfm_t.ap[0][0] + ti * 512 + kc * 128,
                                  [[kfm_t.ap[0][0], 16], [1, 128]])
                        r_sc = AP(qfm_t.tensor, qfm_t.offset + base * qfm_t.ap[0][0] + ti * 512 + sp_lo,
                                  [[qfm_t.ap[0][0], 16], [1, nq]])
                        tsc = kb.emit("tensor", lambda t, l=l_sc, r=r_sc, nq=nq, tp=tp, sbank=sbank:
                                      MM(t, ps[sbank][:, 0:nq], l, r, True, True, tp), inc="T")
                        kb.wait("scalar", "T", tsc)
                        tsq = kb.emit("scalar", lambda s_, nq=nq, sbank=sbank, sqo=sqo: s_.activation(
                            sq_b[:, sqo:sqo + nq], ps[sbank][:, 0:nq], AF.Square, bias=1.0,
                            scale=1.4142135623730951), inc="S")
                        tsq_hist.append(tsq)
                        kb.wait("vector", "S", tsq)
                        has_diag = (sp_lo == kc * 128)
                        so = strip_off[kc]

                        def poly(v, nq=nq, has_diag=has_diag, so=so, sqo=sqo):
                            ins = None
                            col = 0
                            if has_diag:
                                ins = v.scalar_tensor_tensor(atm_b[:, so:so + 128], sq_b[:, sqo:sqo + 128], 1.0,
                                                             maskh_b[:], ALU.add, ALU.mult)
                                col = 128
                            if col < nq:
                                ins = v.tensor_scalar(atm_b[:, so + col:so + nq], sq_b[:, sqo + col:sqo + nq],
                                                      0.5, 0.5, ALU.mult, ALU.add)
                            return ins
                        tat = kb.emit("vector", poly, inc="V")
                        tat_hist.append(tat)
                    kb.wait("tensor", "V", tat)
                    last_mm = None
                    for qi, qc in enumerate(qcs):
                        o = num_ap(h, qi)
                        mms = []
                        lx0 = ap3(qfx2_b, (qc * H + h) * 256, [[1, 128]])
                        lx1 = ap3(qfx2_b, (qc * H + h) * 256 + 128, [[1, 128]])
                        mms.append((lx0, ap3(s0b, h * 65, [[1, 65]])))
                        mms.append((lx1, ap3(s1b, h * 65, [[1, 65]])))
                        lin = ap3(qfm_b, ti * 512 + qc * 128, [[1, 128]])
                        mms.append((lin, ap3(s2stk, ti * 520 + hh * 65, [[1, 65]])))
                        mms.append((ones_row[:, :], ap3(scb, h * 65, [[1, 65]])))
                        for kc in range(qc + 1):
                            so = strip_off[kc] + (qc * 128 - max(kc, qcs[0]) * 128)
                            mms.append((atm_b[:, so:so + 128],
                                        ap3(vaug_b, kc * H * 65 + h * 65, [[1, 65]])))
                        if no_inter:
                            mms = mms[3:]
                        for mi, (l_m, r_m) in enumerate(mms):
                            last_mm = kb.emit("tensor", lambda t, o=o, l=l_m, r=r_m,
                                              st=(mi == 0), sp=(mi == len(mms) - 1):
                                              MM(t, o, l, r, st, sp),
                                              inc=("T" if mi == len(mms) - 1 else None))
                    kb.wait("vector", "T", last_mm)
                tnum = kb.emit("tensor", lambda t: t.nop(), inc="T")
                kb.wait("vector", "T", tnum)
                if bc == 4 and pair == 0 and ("num_a" in taps or "num_7" in taps):
                    kb.wait("scalar", "T", tnum)
                    sda = kb.emit("scalar", lambda s_: s_.copy(dump_b[:, 0:462], ps[3][:, 0:462]), inc="S")
                    kb.wait("gpsimd", "S", sda)
                    tap_now("num_a", dump_b[:, 0:462], (128, 462))
                    sdb = kb.emit("scalar", lambda s_: s_.copy(dump_b[:, 0:264], ps[7][:, 0:264]), inc="S")
                    kb.wait("gpsimd", "S", sdb)
                    tap_now("num_7", dump_b[:, 0:264], (128, 264))
                for qi, qc in enumerate(qcs):
                    bA = 3 if qi == 0 else 5
                    bB = 4 if qi == 0 else 6
                    rq = 0
                    if qi == 1:
                        kb.wait("vector", "V", yv)
                    kb.emit("vector", lambda v, bA=bA, rq=rq: v.tensor_scalar(
                        recip_s[:, rq:rq + 7], ap3(ps[bA], 64, [[66, 7]]), EPS_ATTN, None, ALU.add))
                    kb.emit("vector", lambda v, bB=bB, rq=rq: v.tensor_scalar(
                        recip_s[:, rq + 7:rq + 14], ap3(ps[bB], 64, [[66, 7]]), EPS_ATTN, None, ALU.add))
                    tdn = kb.emit("vector", lambda v, qi=qi, rq=rq: v.tensor_scalar(
                        recip_s[:, rq + 14:rq + 16], ap3(ps[7], qi * 132 + 64, [[66, 2]]), EPS_ATTN, None, ALU.add),
                        inc="V")
                    kb.wait("vector", "V", tdn)
                    trc = kb.emit("vector", lambda v, rq=rq: v.reciprocal(
                        recip_s[:, rq:rq + 16], recip_s[:, rq:rq + 16]), inc="V")
                    kb.wait("vector", "V", trc)
                    trip = ((ap3(y_b, qc * D, [[64, 7], [1, 64]]),
                             ap3(ps[bA], 0, [[66, 7], [1, 64]]),
                             ap3(recip_s, rq, [[1, 7], [0, 64]])),
                            (ap3(y_b, qc * D + 7 * 64, [[64, 7], [1, 64]]),
                             ap3(ps[bB], 0, [[66, 7], [1, 64]]),
                             ap3(recip_s, rq + 7, [[1, 7], [0, 64]])),
                            (ap3(y_b, qc * D + 14 * 64, [[64, 2], [1, 64]]),
                             ap3(ps[7], qi * 132, [[66, 2], [1, 64]]),
                             ap3(recip_s, rq + 14, [[1, 2], [0, 64]])))
                    for o_y, i_n, r_r in trip:
                        yv = kb.emit("vector", lambda v, o=o_y, i=i_n, r=r_r:
                                     v.scalar_tensor_tensor(o, i, 1.0, r, ALU.mult, ALU.mult),
                                     inc="V")
                    if bc == 4 and pair == 0 and qi == 0 and "rcp_q0" in taps:
                        tap_now("rcp_q0", recip_s[:, 0:16], (128, 16), "V")
                kb.wait("tensor", "V", yv)
                if bc == 4 and pair == 0 and "y_p0" in taps:
                    tap_now("y_p0", ap3(y_b, 0, [[1, 2048]]), (128, 2048), "V")
            if bc == 4:
                tap_now("y_b4", ap3(y_b, 0, [[1, 4096]]), (128, 4096), "V")

        kb.mark(f"scan{bc}")
        kb.wait("tensor", "V", v_after_gen)
        kb.wait("tensor", "G", g_after_gen)
        wave_id = 0
        sv3_hist = []
        for fc in range(4):
            for h0, hn in ((0, 7), (7, 7), (14, 2)):
                bank = 3 if (wave_id % 2 == 0) else 4
                wave_id += 1
                if len(sv3_hist) >= 2:
                    kb.wait("tensor", "V", sv3_hist[-2])
                tds = None
                for sub in range(NSUB):
                    for j in range(hn):
                        h = h0 + j
                        if fc < 2:
                            lhs = ap3(kft_b, (sub * H + h) * DE + fc * 128, [[1, 128]])
                            o = ps[bank][:, j * 66:j * 66 + 65]
                        elif fc == 2:
                            lhs = ap3(kft_b, (sub * H + h) * DE + 256, [[1, 16]])
                            o = ps[bank][0:16, j * 66:j * 66 + 65]
                        else:
                            lhs = ap3(kft_b, (sub * H + h) * DE + 272, [[1, 1]])
                            o = ps[bank][0:1, j * 66:j * 66 + 65]
                        r_v = ap3(vaug_b, sub * H * 65 + h * 65, [[1, 65]])
                        st = (sub == 0 and j == 0)
                        sp = (sub == NSUB - 1 and j == hn - 1)
                        tds = kb.emit("tensor", lambda t, o=o, l=lhs, r=r_v, st=st, sp=sp:
                                      MM(t, o, l, r, st, sp), inc=("T" if sp else None))
                kb.wait("vector", "T", tds)
                if fc < 2:
                    dst = s0f if fc == 0 else s1f
                    i_p = ap3(ps[bank], 0, [[66, hn], [1, 65]])
                    o_s3 = ap3(dst, h0 * 65, [[65, hn], [1, 65]])
                elif fc == 2:
                    dst = s2f
                    i_p = AP(ps[bank].tensor, ps[bank].offset, [[ps[bank].ap[0][0], 16], [66, hn], [1, 65]])
                    o_s3 = AP(s2f.tensor, s2f.offset + h0 * 65, [[s2f.ap[0][0], 16], [65, hn], [1, 65]])
                else:
                    i_p = AP(ps[bank].tensor, ps[bank].offset, [[ps[bank].ap[0][0], 1], [66, hn], [1, 65]])
                    o_s3 = AP(scf.tensor, scf.offset + h0 * 65, [[scf.ap[0][0], 1], [65, hn], [1, 65]])
                sv3 = kb.emit("vector", lambda v, o=o_s3, i=i_p: v.tensor_tensor(
                    o, o, i, ALU.add), inc="V")
                sv3_hist.append(sv3)
        kb.wait("tensor", "V", sv3_hist[-1])
        t_after_ds = kb.now("T")

        kb.mark(f"state{bc}")
        if own:
            for sub in range(NSUB):
                co = (bc - NBIG // 2) * NSUB + sub
                xr = sub % 2
                if co - 2 in v_after_h:
                    kb.wait("sync", "V", v_after_h[co - 2])
                xts = "XT0" if co % 2 == 0 else "XT1"
                dxt = kb.dma("sync", xtok_b[:, xr * D:(xr + 1) * D],
                             dram["x_tok"][co * 128:(co + 1) * 128, :], xts)
                for j4 in range(2):
                    for j in range(4):
                        jj = j4 * 4 + j
                        i_y = ap3(y_b, sub * D + jj * 128, [[1, 128]])
                        tv4 = kb.emit("tensor", lambda t, i=i_y, j=j: t.transpose(
                            psb[3][:, j * 128:(j + 1) * 128], i, ident_b[:]),
                            inc=("T" if j == 3 else None))
                    kb.wait("scalar", "T", tv4)
                    o_yt = ap3(yT_b, j4 * 512, [[1, 512]])
                    sv4 = kb.emit("scalar", lambda s_, o=o_yt: s_.copy(o, psb[3][:, 0:512]), inc="S")
                    kb.wait("tensor", "S", sv4)
                for ic in range(8):
                    st, sp = (ic == 0), (ic == 7)
                    l_y = ap3(yT_b, ic * 128, [[1, 128]])
                    kb.emit("tensor", lambda t, l=l_y, ic=ic, st=st, sp=sp: MM(
                        t, ps[5][:], l, ap3(wo_b, ic * D, [[1, 512]]), st, sp))
                    two = kb.emit("tensor", lambda t, l=l_y, ic=ic, st=st, sp=sp: MM(
                        t, ps[6][:], l, ap3(wo_b, ic * D + 512, [[1, 512]]), st, sp),
                        inc=("T" if sp else None))
                kb.wait("vector", "T", two)
                kb.wait("vector", xts, dxt)
                if co - 2 in ds_after_x1:
                    kb.wait("vector", "X1W0" if (co - 2) % 2 == 0 else "X1W1", ds_after_x1[co - 2])
                hr = sub % 2
                o_h0 = h_b[:, hr * D: hr * D + 512]
                o_h1 = h_b[:, hr * D + 512: (hr + 1) * D]
                kb.emit("vector", lambda v, o=o_h0, xr=xr: v.tensor_tensor(
                    o, xtok_b[:, xr * D: xr * D + 512], ps[5][:], ALU.add))
                hv = kb.emit("vector", lambda v, o=o_h1, xr=xr: v.tensor_tensor(
                    o, xtok_b[:, xr * D + 512:(xr + 1) * D], ps[6][:], ALU.add), inc="V")
                v_after_h[co] = hv
                kb.wait("tensor", "V", hv)
                if co == 0:
                    tap_now("h_c0", h_b[:, 0:1024], (128, 1024), "V")
                vap = ln_tile(h_b[:, hr * D:(hr + 1) * D], 0,
                              h_b[:, hr * D:(hr + 1) * D], None)
                kb.wait("sync", "V", vap)
                ds_after_x1[co] = kb.dma("sync", x1_scr[co * 128:(co + 1) * 128, :],
                                         h_b[:, hr * D:(hr + 1) * D],
                                         "X1W0" if co % 2 == 0 else "X1W1")

    kb.mark("attn")
    kb.wait("sync", "X1W0", kb.now("X1W0"))
    kb.wait("sync", "X1W1", kb.now("X1W1"))
    t_x1tr = {}
    vx1_by_co = {}
    for co in range(16):
        xr = co % 2
        x1s = "X1R0" if co % 2 == 0 else "X1R1"
        if co - 2 in vx1_by_co:
            kb.wait("sync", "V", vx1_by_co[co - 2])
        dx1 = kb.dma("sync", xtok_b[:, xr * D:(xr + 1) * D],
                     x1_scr[co * 128:(co + 1) * 128, :], x1s)
        kb.wait("vector", x1s, dx1)
        if co - 2 in t_x1tr:
            kb.wait("vector", "T", t_x1tr[co - 2])
        vx1 = kb.emit("vector", lambda v, xr=xr: v.tensor_copy(
            x1bf_b[:, xr * D:(xr + 1) * D], xtok_b[:, xr * D:(xr + 1) * D]), inc="V")
        vx1_by_co[co] = vx1
        kb.wait("tensor", "V", vx1)
        for j4 in range(2):
            for j in range(4):
                jj = j4 * 4 + j
                i_x = x1bf_b[:, xr * D + jj * 128: xr * D + (jj + 1) * 128]
                tvx = kb.emit("tensor", lambda t, i=i_x, j=j: t.transpose(
                    psb[3][:, j * 128:(j + 1) * 128], i, ident_b[:]),
                    inc=("T" if j == 3 else None))
            t_x1tr[co] = tvx
            kb.wait("scalar", "T", tvx)
            o_fm = ap3(x1fm_b, (j4 * 4) * OWN + co * 128, [[OWN, 4], [1, 128]])
            svx = kb.emit("scalar", lambda s_, o=o_fm: s_.copy(
                o, psb[3][:, 0:512].rearrange("p (g t) -> p g t", g=4)), inc="S")
            kb.wait("tensor", "S", svx)

    add_tap("x1fm", ap3(x1fm_b, 0, [[1, 2048]]), (128, 2048))

    kb.mark("x1fm")
    t_after_moe_e = {}
    t_after_swg = {}
    t_after_sw2 = {}
    v_after_out = {}
    do_after_out = {}
    v_x1bf_last = kb.now("V")
    for oc in range(8):
        o_i = ap3(omoe_b, oc * 2048, [[8, 256], [1, 8]])
        i_i = ap3(b2p_s, oc * 8, [[0, 256], [1, 8]])
        kb.emit("vector", lambda v, o=o_i, i=i_i: v.tensor_copy(o, i))
    vini = kb.emit("vector", lambda v: v.engine_nop(), inc="V")
    kb.wait("tensor", "V", vini)
    tap_now("om_init", ap3(omoe_b, 0, [[1, 1024]]), (128, 1024), "V")
    sh_hist = []
    vw2_hist = []
    for e in range(NE):
        if e - 1 in t_after_moe_e:
            kb.wait("gpsimd", "T", t_after_moe_e[e - 1])
        else:
            kb.wait("gpsimd", "T", kb.now("T"))
        dw1 = kb.dma("gpsimd", ap3(w1_b, 0, [[D, 8], [1, D]]),
                     dram["w1"][e].rearrange("(c p) n -> p c n", p=128), "WM1")
        dw2 = kb.dma("gpsimd", ap3(w2_b, 0, [[D, 8], [1, D]]),
                     dram["w2"][e].rearrange("(c p) n -> p c n", p=128), "WM2")
        lo, delta = MOE_RES[e]
        kb.wait("tensor", "WM1", dw1)
        kb.wait("scalar", "T", kb.now("T"))
        for hf in range(8):
            bank = hf % 2
            if len(sh_hist) >= 2:
                kb.wait("tensor", "S", sh_hist[-2])
            for ic in range(8):
                rhs = ap3(x1fm_b, ic * OWN + lo, [[8, 256], [delta, 2]])
                l_w = ap3(w1_b, ic * D + hf * 128, [[1, 128]])
                o_h = ps[bank][:, 0:512].rearrange("p (g r) -> p g r", g=256)
                th = kb.emit("tensor", lambda t, o=o_h, l=l_w, r=rhs, ic=ic:
                             MM(t, o, l, r, ic == 0, ic == 7), inc=("T" if ic == 7 else None))
            kb.wait("scalar", "T", th)
            sh = kb.emit("scalar", lambda s_, e=e, hf=hf, bank=bank: s_.activation(
                hmoe_b[:, hf * 512:(hf + 1) * 512], ps[bank][:, 0:512], AF.Gelu,
                bias=b1_s[:, e * 8 + hf: e * 8 + hf + 1], scale=1.0), inc="S")
            sh_hist.append(sh)
        if e == 0:
            kb.wait("gpsimd", "S", sh_hist[-1])
            tap_now("hm_e0", hmoe_b[:, 0:2048], (128, 2048), "S")
        kb.wait("tensor", "WM2", dw2)
        kb.wait("tensor", "S", sh_hist[-1])
        for oc in range(8):
            bank = 2 + oc % 2
            if len(vw2_hist) >= 2:
                kb.wait("tensor", "V", vw2_hist[-2])
            for hf in range(8):
                l_w = ap3(w2_b, hf * D + oc * 128, [[1, 128]])
                tw2 = kb.emit("tensor", lambda t, l=l_w, hf=hf, bank=bank: MM(
                    t, ps[bank][:, 0:512], l, hmoe_b[:, hf * 512:(hf + 1) * 512],
                    hf == 0, hf == 7), inc=("T" if hf == 7 else None))
            kb.wait("vector", "T", tw2)
            o_m = ap3(omoe_b, oc * 2048 + lo, [[8, 256], [delta, 2]])
            i_m = ps[bank][:, 0:512].rearrange("p (g r) -> p g r", g=256)
            vw2 = kb.emit("vector", lambda v, o=o_m, i=i_m: v.scalar_tensor_tensor(
                o, i, 0.25, o, ALU.mult, ALU.add), inc="V")
            vw2_hist.append(vw2)
        t_after_moe_e[e] = kb.now("T")
    tap_now("om_moe", ap3(omoe_b, 0, [[1, 1024]]), (128, 1024), "V")

    kb.mark("moe")
    kb.wait("tensor", "S", sh_hist[-1])
    kb.wait("tensor", "V", vw2_hist[-1])
    tsg_hist = []
    tf_hist = []
    vtf_hist = []
    sq2_b = bfv("sq2", 512)
    for half in range(2):
        tok0 = half * 1024
        for g in range(4):
            gr = g % 2
            key = (half, g - 2)
            if key in t_after_swg:
                kb.wait("gpsimd", "T", t_after_swg[key])
            elif g < 2:
                kb.wait("gpsimd", "T",
                        t_after_moe_e[NE - 1] if half == 0 else t_after_swg[(0, 3)])
            sws = "SWE" if g % 2 == 0 else "SWO"
            dsw1 = kb.dma("gpsimd", ap3(sw1_b, gr * 4096, [[512, 8], [1, 512]]),
                          dram["sw1"][:, g * 512:(g + 1) * 512].rearrange("(c p) n -> p c n", p=128), sws)
            dsw3 = kb.dma("gpsimd", ap3(sw3_b, gr * 4096, [[512, 8], [1, 512]]),
                          dram["sw3"][:, g * 512:(g + 1) * 512].rearrange("(c p) n -> p c n", p=128), sws)
            kb.wait("tensor", sws, dsw3)
            for hfl in range(4):
                hf = g * 4 + hfl
                for tt in range(2):
                    bg = tt % 2
                    bu = 2 + tt % 2
                    if len(tsg_hist) >= 2:
                        kb.wait("tensor", "V", tsg_hist[-2])
                        kb.wait("scalar", "V", tsg_hist[-2])
                    for ic in range(8):
                        rhs = ap3(x1fm_b, ic * OWN + tok0 + tt * 512, [[1, 512]])
                        l1 = ap3(sw1_b, gr * 4096 + ic * 512 + hfl * 128, [[1, 128]])
                        l3 = ap3(sw3_b, gr * 4096 + ic * 512 + hfl * 128, [[1, 128]])
                        kb.emit("tensor", lambda t, l=l1, r=rhs, ic=ic, bg=bg:
                                MM(t, ps[bg][:], l, r, ic == 0, ic == 7))
                        tgu = kb.emit("tensor", lambda t, l=l3, r=rhs, ic=ic, bu=bu:
                                      MM(t, ps[bu][:], l, r, ic == 0, ic == 7),
                                      inc=("T" if ic == 7 else None))
                    kb.wait("scalar", "T", tgu)
                    sq_t = sq_b[:, 0:512] if tt == 0 else sq2_b[:, 0:512]
                    du_t = dump_b[:, 0:512] if tt == 0 else dump_b[:, 512:1024]
                    tsi = kb.emit("scalar", lambda s_, hf=hf, bg=bg, o=sq_t: s_.activation(
                        o, ps[bg][:], AF.Silu,
                        bias=sb1_s[:, hf:hf + 1], scale=1.0), inc="S")
                    kb.wait("vector", "S", tsi)
                    kb.emit("vector", lambda v, hf=hf, bu=bu, o=du_t: v.tensor_scalar(
                        o, ps[bu][:], sb3_s[:, hf:hf + 1], 0.5, ALU.add, ALU.mult))
                    o_sg = ap3(sgate_b, hf * 1024 + tt * 512, [[1, 512]])
                    tsg = kb.emit("vector", lambda v, o=o_sg, a=sq_t, b=du_t: v.tensor_tensor(
                        o, a, b, ALU.mult), inc="V")
                    tsg_hist.append(tsg)
                    if half == 0 and hf == 0 and tt == 1:
                        kb.wait("tensor", "V", tsg)
                        tap_now("sg_0", ap3(sgate_b, 0, [[1, 1024]]), (128, 1024), "V")
            t_after_swg[(half, g)] = kb.now("T")
        kb.mark(f"sg{half}")
        kb.wait("tensor", "V", tsg_hist[-1])
        for oc in range(8):
            ocr = oc % 2
            key = (half, oc - 2)
            if key in t_after_sw2:
                kb.wait("gpsimd", "T", t_after_sw2[key])
            elif oc < 2:
                kb.wait("gpsimd", "T",
                        t_after_moe_e[NE - 1] if half == 0 else t_after_sw2[(0, 7)])
            w2sem = "W2E" if oc % 2 == 0 else "W2O"
            dsw2 = kb.dma("gpsimd", ap3(sw2_b, ocr * 2048, [[128, 16], [1, 128]]),
                          dram["sw2"][:, oc * 128:(oc + 1) * 128].rearrange("(c p) n -> p c n", p=128), w2sem)
            kb.wait("tensor", w2sem, dsw2)
            for tt in range(2):
                bank = 4 + tt % 2
                if len(tf_hist) >= 2:
                    kb.wait("tensor", "V", tf_hist[-2])
                for hf in range(16):
                    l_w = ap3(sw2_b, ocr * 2048 + hf * 128, [[1, 128]])
                    r_s = ap3(sgate_b, hf * 1024 + tt * 512, [[1, 512]])
                    tsw2 = kb.emit("tensor", lambda t, l=l_w, r=r_s, hf=hf, bank=bank:
                                   MM(t, ps[bank][:], l, r, hf == 0, hf == 15),
                                   inc=("T" if hf == 15 else None))
                kb.wait("vector", "T", tsw2)
                o_f = ap3(omoe_b, oc * 2048 + tok0 + tt * 512, [[1, 512]])
                tf = kb.emit("vector", lambda v, o=o_f, oc=oc, bank=bank: v.scalar_tensor_tensor(
                    o, ps[bank][:], sb2h_s[:, oc:oc + 1], o, ALU.add, ALU.add), inc="V")
                tf_hist.append(tf)
            if half == 0 and oc == 0:
                tap_now("f_0", ap3(omoe_b, 0, [[1, 1024]]), (128, 1024), "V")
            t_after_sw2[(half, oc)] = kb.now("T")
        kb.mark(f"sw2{half}")
        kb.wait("tensor", "V", tf_hist[-1])
        for ct in range(8):
            co = half * 8 + ct
            xr = ct % 2
            if co - 2 in v_after_out:
                kb.wait("sync", "V", v_after_out[co - 2])
            elif half == 0 and ct < 2:
                kb.wait("sync", "V", v_x1bf_last)
            x1s = "X1R0" if co % 2 == 0 else "X1R1"
            dx1 = kb.dma("sync", xtok_b[:, xr * D:(xr + 1) * D],
                         x1_scr[co * 128:(co + 1) * 128, :], x1s)
            kb.wait("vector", x1s, dx1)
            hr = ct % 2
            for j in range(8):
                bank = 6 + j % 2
                if len(vtf_hist) >= 2:
                    kb.wait("tensor", "V", vtf_hist[-2])
                i_f = ap3(omoe_b, j * 2048 + co * 128, [[1, 128]])
                tfj = kb.emit("tensor", lambda t, i=i_f, bank=bank: t.transpose(
                    ps[bank][:, 0:128], i, ident_f[:]), inc="T")
                kb.wait("vector", "T", tfj)
                o_h = h_b[:, hr * D + j * 128: hr * D + (j + 1) * 128]
                i_x = xtok_b[:, xr * D + j * 128: xr * D + (j + 1) * 128]
                vtf = kb.emit("vector", lambda v, o=o_h, i=i_x, bank=bank: v.tensor_tensor(
                    o, i, ps[bank][:, 0:128], ALU.add), inc="V")
                vtf_hist.append(vtf)
            if co - 2 in do_after_out:
                kb.wait("vector", "DO0" if (co - 2) % 2 == 0 else "DO1", do_after_out[co - 2])
            if half == 0 and ct == 0:
                tap_now("h2_0", h_b[:, 0:1024], (128, 1024), "V")
            vo = ln_tile(h_b[:, hr * D:(hr + 1) * D], 1,
                         x2o_b[:, xr * D:(xr + 1) * D], None)
            v_after_out[co] = vo
            kb.wait("sync", "V", vo)
            do_after_out[co] = kb.dma("sync", out_d[co * 128:(co + 1) * 128, :],
                                      x2o_b[:, xr * D:(xr + 1) * D],
                                      "DO0" if co % 2 == 0 else "DO1")

    for t, src_ap in tap_dmas:
        kb.dma("gpsimd", t, src_ap, "DTAP")
    kb.wait("sync", "DO0", kb.now("DO0"))
    kb.wait("sync", "DO1", kb.now("DO1"))
    kb.wait("sync", "DTAP", kb.now("DTAP"))

    if cut is not None:
        kb.cut_to(cut)
        kb.sem("DCUT")
        for co in range(16):
            kb.dma("sync", out_d[co * 128:(co + 1) * 128, :],
                   x2o_b[:, (co % 2) * D:((co % 2) + 1) * D], "DCUT")
        kb.wait("sync", "DCUT", kb.now("DCUT"))

    with ctx:
        with nc.Block() as block:
            @block.sync
            def _(sync):
                for f in kb.ops["sync"]:
                    f(sync)

            @block.tensor
            def _(tns):
                for f in kb.ops["tensor"]:
                    f(tns)

            @block.vector
            def _(vec):
                for f in kb.ops["vector"]:
                    f(vec)

            @block.scalar
            def _(scl):
                for f in kb.ops["scalar"]:
                    f(scl)

            @block.gpsimd
            def _(gps):
                for f in kb.ops["gpsimd"]:
                    f(gps)
    return nc


def stage_inputs(inputs):
    x = np.ascontiguousarray(np.asarray(inputs["L0_attn_hidden_states"], dtype=np.float32))
    Wq = np.asarray(inputs["L0_attn_Wq"], np.float32)
    Wk = np.asarray(inputs["L0_attn_Wk"], np.float32)
    Wv = np.asarray(inputs["L0_attn_Wv"], np.float32)
    Wo = np.asarray(inputs["L0_attn_Wo"], np.float32)
    wcat = np.ascontiguousarray(np.concatenate([Wq.T, Wk.T, Wv.T], axis=1))
    wo_in = np.ascontiguousarray(Wo.T)
    sw1 = np.ascontiguousarray(np.asarray(inputs["L0_ffn_sw1"], np.float32))
    sw3 = np.ascontiguousarray(np.asarray(inputs["L0_ffn_sw3"], np.float32))
    sw2 = np.ascontiguousarray(np.asarray(inputs["L0_ffn_sw2"], np.float32))
    sb1 = np.ascontiguousarray(np.asarray(inputs["L0_ffn_sb1"], np.float32).reshape(16, 128).T)
    sb3 = np.ascontiguousarray(np.asarray(inputs["L0_ffn_sb3"], np.float32).reshape(16, 128).T)
    sb2 = np.asarray(inputs["L0_ffn_sb2"], np.float32)
    sb2h = np.ascontiguousarray((0.5 * sb2).reshape(8, 128).T)
    w1 = np.ascontiguousarray(np.asarray(inputs["L0_ffn_w1"], np.float32))
    b1 = np.asarray(inputs["L0_ffn_b1"], np.float32)
    b1r = np.ascontiguousarray(b1.reshape(NE, 8, 128).transpose(2, 0, 1))
    w2 = np.ascontiguousarray(np.asarray(inputs["L0_ffn_w2"], np.float32))
    b2 = np.asarray(inputs["L0_ffn_b2"], np.float32)
    b2p = np.zeros((128, 8, 8), np.float32)
    for r in range(8):
        e0 = (3 * r) % 8
        e1 = (3 * r + 7) % 8
        b2p[:, :, r] = (0.25 * (b2[e0] + b2[e1])).reshape(8, 128).T
    maskh = np.ascontiguousarray(np.triu(np.ones((128, 128), np.float32)) * 0.5)
    ident = np.eye(128, dtype=np.float32)
    rep4 = np.zeros((16, 128), np.float32)
    for g in range(4):
        rep4[np.arange(16), 32 * g + np.arange(16)] = 1.0
    sel16 = np.zeros((16, 1024), np.float32)
    for j in range(8):
        sel16[np.arange(16), j * 128 + 16 * j + np.arange(16)] = 1.0

    per_core = []
    for b in range(4):
        for half in range(2):
            if half == 0:
                win = np.concatenate([np.zeros((2048, D), np.float32), x[b, :2048]], axis=0)
                val = np.concatenate([np.zeros(16, np.float32), np.ones(16, np.float32)])
            else:
                win = x[b]
                val = np.ones(32, np.float32)
            per_core.append({
                "x_fm": np.ascontiguousarray(win.T),
                "x_tok": np.ascontiguousarray(x[b, half * 2048:(half + 1) * 2048]),
                "valid": np.ascontiguousarray(np.broadcast_to(val[None, :], (128, NCH))),
                "wcat": wcat, "wo_in": wo_in,
                "sw1": sw1, "sw3": sw3, "sw2": sw2,
                "sb1": sb1, "sb3": sb3, "sb2h": sb2h,
                "w1": w1, "b1": b1r, "b2p": b2p, "w2": w2,
                "maskh": maskh, "ident": ident, "rep4": rep4, "sel16": sel16,
            })
    return per_core


def kernel(**inputs):
    per_core = stage_inputs(inputs)
    nc = build_program()
    res = run_bass_kernel_spmd(nc, per_core, core_ids=list(range(8)))
    out = np.empty((4, 4096, 1024), np.float32)
    for b in range(4):
        for half in range(2):
            out[b, half * 2048:(half + 1) * 2048] = res.results[b * 2 + half]["out"]
    return out


# version 09 (final)
# speedup vs baseline: 1.2788x; 1.0268x over previous
import numpy as np
import concourse.bass as bass
import concourse.mybir as mybir
from concourse.ap import AP
from concourse.bass_utils import run_bass_kernel_spmd

F32 = mybir.dt.float32
BF16 = mybir.dt.bfloat16
ALU = mybir.AluOpType
AF = mybir.ActivationFunctionType
AXX = mybir.AxisListType.X

D = 1024
H = 16
FT = 16
WIN = 4096
OWN = 2048
NCH = 32
NBIG = 8
NSUB = 4
DE = 273
HS = 2048
NE = 8
EPS_ATTN = 1e-12
EPS_LN = 1e-5

MOE_RES = []
for e in range(NE):
    r0 = (3 * e) % 8
    r1 = (3 * e + 3) % 8
    MOE_RES.append((min(r0, r1), abs(r1 - r0)))


class KB:
    ENGINES = ("sync", "tensor", "vector", "scalar", "gpsimd")

    def __init__(self, nc):
        self.nc = nc
        self.ops = {e: [] for e in self.ENGINES}
        self.sems = {}
        self.cnt = {}

    def sem(self, name):
        self.sems[name] = self.nc.alloc_semaphore(name)
        self.cnt[name] = 0

    def emit(self, eng, fn, inc=None):
        if inc is not None:
            self.cnt[inc] += 1
            sem = self.sems[inc]
            self.ops[eng].append(lambda e, fn=fn, sem=sem: fn(e).then_inc(sem, 1))
            return self.cnt[inc]
        self.ops[eng].append(fn)
        return None

    def wait(self, eng, sem_name, val):
        if val is None or val <= 0:
            return
        sem = self.sems[sem_name]
        self.ops[eng].append(lambda e, sem=sem, v=val: e.wait_ge(sem, v))

    def now(self, s):
        return self.cnt[s]

    def mark(self, name):
        self.marks = getattr(self, "marks", {})
        self.marks[name] = {e: len(self.ops[e]) for e in self.ENGINES}

    def cut_to(self, name):
        m = self.marks[name]
        for e in self.ENGINES:
            self.ops[e] = self.ops[e][:m[e]]

    def dma(self, eng, out, in_, sem_name):
        self.cnt[sem_name] += 16
        sem = self.sems[sem_name]
        self.ops[eng].append(
            lambda e, out=out, in_=in_, sem=sem: e.dma_start(out=out, in_=in_).then_inc(sem, 16))
        return self.cnt[sem_name]


def ap3(t_ap, base_off, dims):
    return AP(t_ap.tensor, t_ap.offset + base_off,
              [list(t_ap.ap[0])] + [list(d) for d in dims])


def build_program(taps=(), cut=None, no_inter=False, races=False):
    nc = bass.Bass("TRN2", target_bir_lowering=False, detect_race_conditions=races)
    dram = {}

    def din(name, shape):
        dram[name] = nc.dram_tensor(name, list(shape), F32, kind="ExternalInput").ap()
        return dram[name]

    din("x_fm", (D, WIN))
    din("x_tok", (OWN, D))
    din("valid", (128, NCH))
    din("wcat", (D, 1536))
    din("wo_in", (D, D))
    din("sw1", (D, HS))
    din("sw3", (D, HS))
    din("sw2", (HS, D))
    din("sb1", (128, 16))
    din("sb3", (128, 16))
    din("sb2h", (128, 8))
    din("w1", (NE, D, D))
    din("b1", (128, NE, 8))
    din("b2p", (128, 8, 8))
    din("w2", (NE, D, D))
    din("maskh", (128, 128))
    din("ident", (128, 128))
    din("rep4", (16, 128))
    din("sel16", (16, 1024))
    out_d = nc.dram_tensor("out", [OWN, D], F32, kind="ExternalOutput").ap()
    x1_scr = nc.dram_tensor("x1_scr", [OWN, D], F32, kind="Internal").ap()

    from contextlib import ExitStack
    ctx = ExitStack()

    def sb(name, shape, dt=BF16):
        return ctx.enter_context(nc.sbuf_tensor(name, list(shape), dt)).ap()

    maskh_b = sb("maskh_b", (128, 128))
    ident_b = sb("ident_b", (128, 128))
    ident_f = sb("ident_f", (128, 128), F32)
    rep4_b = sb("rep4_b", (16, 128))
    sel16_b = sb("sel16_b", (16, 1024))
    ones_row = sb("ones_row", (1, 128))
    valid_s = sb("valid_s", (128, NCH), F32)
    sb1_s = sb("sb1_s", (128, 16), F32)
    sb3_s = sb("sb3_s", (128, 16), F32)
    sb2h_s = sb("sb2h_s", (128, 8), F32)
    b1_s = sb("b1_s", (128, NE * 8), F32)
    b2p_s = sb("b2p_s", (128, NE * 8), F32)
    xtok_b = sb("xtok_b", (128, 2 * D), F32)
    h_b = sb("h_b", (128, 2 * D), F32)
    dump_b = sb("dump_b", (128, D), F32)
    x1bf_b = sb("x1bf_b", (128, 2 * D))
    x2o_b = sb("x2o_b", (128, 2 * D), F32)
    m_b = sb("m_b", (128, 2), F32)
    mneg_b = sb("mneg_b", (128, 2), F32)
    var_b = sb("var_b", (128, 2), F32)
    rstd_b = sb("rstd_b", (128, 2), F32)
    recip_s = sb("recip_s", (128, 16), F32)
    eps_b = sb("eps_b", (128, 1), F32)

    AR_BYTES = 175424
    AR = sb("arena", (128, AR_BYTES // 4), F32)
    ARB = AR.bitcast(BF16)

    off = {}

    def lay(name, nbytes, at=None):
        if at is None:
            at = lay.cur
            lay.cur += (nbytes + 63) // 64 * 64
        off[name] = at
        assert at + nbytes <= AR_BYTES, (name, at, nbytes)
        return at
    lay.cur = 0

    def bfv(name, cols, p0=0, np_=128):
        o = off[name]
        assert o % 2 == 0
        a = ARB[:, o // 2: o // 2 + cols]
        if p0 or np_ != 128:
            a = AP(a.tensor, a.offset + p0 * a.ap[0][0], [[a.ap[0][0], np_]] + [list(d) for d in a.ap[1:]])
        return a

    def f32v(name, cols, p0=0, np_=128):
        o = off[name]
        assert o % 4 == 0
        a = AR[:, o // 4: o // 4 + cols]
        if p0 or np_ != 128:
            a = AP(a.tensor, a.offset + p0 * a.ap[0][0], [[a.ap[0][0], np_]] + [list(d) for d in a.ap[1:]])
        return a

    lay("wcat", 8 * 1536 * 2)
    lay("wo", 8 * D * 2)
    kft_at = lay("kft", NSUB * H * DE * 2)
    lay("qfx2", NSUB * H * 256 * 2)
    lay("vaug", NSUB * H * 65 * 2)
    lay("qT", NSUB * 256 * 2)
    lay("kT", NSUB * 256 * 2)
    lay("qfm", 2 * 512 * 2)
    lay("qfmS", 2 * 512 * 2)
    lay("kfm", 2 * 512 * 2)
    lay("kfmS", 2 * 512 * 2)
    lay("yqfT", max(NSUB * D, 4096) * 2)
    lay("yT", 8 * 128 * 2)
    lay("sq", 512 * 2)
    lay("atm", 1024 * 2)
    lay("s0f", H * 65 * 4)
    lay("s1f", H * 65 * 4)
    lay("s2f", H * 65 * 4)
    lay("scf", H * 65 * 4)
    lay("s0b", H * 65 * 2)
    lay("s1b", H * 65 * 2)
    lay("s2b", H * 65 * 2)
    lay("s2b4", H * 65 * 2)
    lay("scb", H * 65 * 2)
    lay("xfm", 2 * 1024 * 2)
    a_end = lay.cur
    assert a_end <= AR_BYTES, a_end

    lay("x1fm", 8 * OWN * 2, at=kft_at)
    lay("w1s", 8 * D * 2, at=0)
    lay("w2s", 8 * D * 2, at=8 * D * 2)
    lay("sw1s", 2 * 8 * 512 * 2, at=0)
    lay("sw3s", 2 * 8 * 512 * 2, at=8 * D * 2)
    lay("hmoe", 8 * 512 * 2, at=16 * D * 2)
    lay("sw2s", 2 * 16 * 128 * 2, at=16 * D * 2)
    f_cur = kft_at + 8 * OWN * 2
    lay("omoe", 8 * 2048 * 4, at=f_cur); f_cur += 8 * 2048 * 4
    f_cur = max(f_cur, 140544)
    lay("sgate", 16 * 1024 * 2, at=f_cur); f_cur += 16 * 1024 * 2
    lay("sq2", 512 * 2, at=f_cur); f_cur += 512 * 2
    assert f_cur <= AR_BYTES, f_cur

    wcat_b = bfv("wcat", 8 * 1536)
    wo_b = bfv("wo", 8 * D)
    kft_b = bfv("kft", NSUB * H * DE)
    qfx2_b = bfv("qfx2", NSUB * H * 256)
    vaug_b = bfv("vaug", NSUB * H * 65)
    qT_b = bfv("qT", NSUB * 256)
    kT_b = bfv("kT", NSUB * 256)
    qfm_b = bfv("qfm", 2 * 512)
    qfmS_b = bfv("qfmS", 2 * 512)
    kfm_b = bfv("kfm", 2 * 512)
    kfmS_b = bfv("kfmS", 2 * 512)
    y_b = bfv("yqfT", NSUB * D)
    qfT_b = bfv("yqfT", 4096)
    yT_b = bfv("yT", 8 * 128)
    sq_b = bfv("sq", 512)
    atm_b = bfv("atm", 1024)
    s0f = f32v("s0f", H * 65)
    s1f = f32v("s1f", H * 65)
    s2f = f32v("s2f", H * 65, 0, 16)
    scf = f32v("scf", H * 65, 0, 1)
    s0b = bfv("s0b", H * 65)
    s1b = bfv("s1b", H * 65)
    s2b = bfv("s2b", H * 65, 0, 16)
    s2stk = bfv("s2b4", H * 65)
    scb = bfv("scb", H * 65, 0, 1)
    xfm_b = bfv("xfm", 2 * 1024)
    x1fm_b = bfv("x1fm", 8 * OWN)
    w1_b = bfv("w1s", 8 * D)
    w2_b = bfv("w2s", 8 * D)
    hmoe_b = bfv("hmoe", 8 * 512)
    omoe_b = f32v("omoe", 8 * 2048)
    sgate_b = bfv("sgate", 16 * 1024)
    sw1_b = bfv("sw1s", 2 * 8 * 512)
    sw3_b = bfv("sw3s", 2 * 8 * 512)
    sw2_b = bfv("sw2s", 2 * 16 * 128)

    ps = [ctx.enter_context(nc.psum_tensor(f"pb{i}", [128, 512], F32)).ap() for i in range(8)]
    psb = [p.bitcast(BF16) for p in ps]

    kb = KB(nc)
    for s in ("T", "V", "S", "G", "DPX", "DPC", "XF0", "XF1", "WM1", "WM2",
              "SWE", "SWO", "W2E", "W2O", "XT0", "XT1", "X1W0", "X1W1",
              "X1R0", "X1R1", "SH", "DO0", "DO1", "DTAP"):
        kb.sem(s)

    kb.mark("zero")
    tap_dmas = []

    def add_tap(name, src_ap, shape):
        if name in taps:
            t = nc.dram_tensor("tap_" + name, list(shape), F32, kind="ExternalOutput").ap()
            tap_dmas.append((t, src_ap))

    tap_fired = set()

    def tap_now(name, src_ap, shape, wait_eng_sem=None):
        if name not in taps or name in tap_fired:
            return
        tap_fired.add(name)
        t = nc.dram_tensor("tap_" + name, list(shape), F32, kind="ExternalOutput").ap()
        if wait_eng_sem is not None:
            kb.wait("gpsimd", wait_eng_sem, kb.now(wait_eng_sem))
        dv = kb.dma("gpsimd", t, src_ap, "DTAP")
        for eng in ("vector", "tensor", "scalar"):
            kb.wait(eng, "DTAP", dv)

    def MM(t, o, l, r, st, sp, tp=None):
        return t.matmul(o, l, r, start=st, stop=sp, tile_position=tp, skip_group_check=True)

    kb.dma("gpsimd", maskh_b[:], dram["maskh"], "DPC")
    kb.dma("gpsimd", ident_b[:], dram["ident"], "DPC")
    kb.dma("gpsimd", rep4_b[:], dram["rep4"], "DPC")
    kb.dma("gpsimd", sel16_b[:], dram["sel16"], "DPC")
    kb.dma("gpsimd", wcat_b.rearrange("p (c n) -> p c n", c=8),
           dram["wcat"].rearrange("(c p) n -> p c n", p=128), "DPC")
    kb.dma("gpsimd", wo_b.rearrange("p (c n) -> p c n", c=8),
           dram["wo_in"].rearrange("(c p) n -> p c n", p=128), "DPC")
    kb.dma("sync", ident_f[:], dram["ident"], "DPX")
    kb.dma("sync", valid_s[:], dram["valid"], "DPX")
    kb.dma("sync", sb1_s[:], dram["sb1"], "DPX")
    kb.dma("sync", sb3_s[:], dram["sb3"], "DPX")
    kb.dma("sync", sb2h_s[:], dram["sb2h"], "DPX")
    kb.dma("sync", b1_s[:].rearrange("p (e c) -> p e c", e=NE), dram["b1"], "DPX")
    kb.dma("sync", b2p_s[:].rearrange("p (c r) -> p c r", c=8), dram["b2p"], "DPX")
    pre_dx, pre_dc = kb.now("DPX"), kb.now("DPC")
    kb.mark("pre_dma")
    kb.wait("vector", "DPX", pre_dx)
    kb.wait("vector", "DPC", pre_dc)
    kb.emit("vector", lambda v: v.memset(ones_row[:], 1.0))
    kb.emit("vector", lambda v: v.memset(eps_b[:], EPS_LN))
    kb.emit("vector", lambda v: v.memset(s0f, 0.0))
    kb.emit("vector", lambda v: v.memset(s1f, 0.0))
    kb.emit("vector", lambda v: v.memset(s2f, 0.0))
    v_pre = kb.emit("vector", lambda v: v.memset(scf, 0.0), inc="V")
    kb.mark("pre_ms")
    kb.wait("tensor", "V", v_pre)
    kb.wait("scalar", "V", v_pre)
    kb.wait("tensor", "DPC", pre_dc)
    kb.wait("tensor", "DPX", pre_dx)
    kb.wait("gpsimd", "V", v_pre)

    kb.mark("pre")
    t_after_proj = {}
    v_after_evac = {}
    ds_after_x1 = {}
    g_after_gen = 0
    v_after_gen = 0
    t_after_ds = 0
    v_after_h = {}

    def ln_tile(src_f32, slot, out_ap, also_bf=None):
        vr = kb.emit("vector", lambda v: v.reduce_sum(m_b[:, slot:slot + 1], src_f32, AXX), inc="V")
        kb.wait("vector", "V", vr)
        vm = kb.emit("vector", lambda v: v.tensor_scalar(
            mneg_b[:, slot:slot + 1], m_b[:, slot:slot + 1], -1.0 / D, None, ALU.mult), inc="V")
        kb.wait("scalar", "V", vm)
        sv = kb.emit("scalar", lambda s_: s_.activation(
            dump_b[:], src_f32, AF.Square, bias=mneg_b[:, slot:slot + 1], scale=1.0), inc="S")
        kb.wait("vector", "S", sv)
        vv2 = kb.emit("vector", lambda v: v.reduce_sum(var_b[:, slot:slot + 1], dump_b[:], AXX), inc="V")
        kb.wait("scalar", "V", vv2)
        sn = kb.emit("scalar", lambda s_: s_.activation(
            rstd_b[:, slot:slot + 1], var_b[:, slot:slot + 1], AF.Ln, bias=eps_b[:], scale=1.0 / D), inc="S")
        kb.wait("scalar", "S", sn)
        sl = kb.emit("scalar", lambda s_: s_.activation(
            rstd_b[:, slot:slot + 1], rstd_b[:, slot:slot + 1], AF.Exp, bias=0.0, scale=-0.5), inc="S")
        kb.wait("vector", "S", sl)
        vap = kb.emit("vector", lambda v: v.tensor_scalar(
            out_ap, src_f32, mneg_b[:, slot:slot + 1], rstd_b[:, slot:slot + 1],
            ALU.add, ALU.mult), inc="V")
        if also_bf is not None:
            vap = kb.emit("vector", lambda v: v.tensor_copy(also_bf, out_ap), inc="V")
        return vap

    tsq_hist = []
    tat_hist = []
    sv2_hist = []
    srep_hist = []
    vv_hist = []
    for bc in range(NBIG):
        own = bc >= NBIG // 2

        for sub in range(NSUB):
            c = bc * NSUB + sub
            xr = c % 2
            if c - 2 in t_after_proj:
                kb.wait("gpsimd", "T", t_after_proj[c - 2])
            src = dram["x_fm"][:, c * 128:(c + 1) * 128].rearrange("(cc p) t -> p cc t", p=128)
            xfs = "XF0" if c % 2 == 0 else "XF1"
            dxv = kb.dma("gpsimd",
                         ap3(xfm_b, xr * 1024, [[128, 8], [1, 128]]),
                         src, xfs)
            kb.wait("tensor", xfs, dxv)
            if sub == 0:
                kb.wait("vector", "G", kb.now("G"))
            P = 0 if c % 2 == 0 else 5
            if sub == 0 and tsq_hist:
                kb.wait("tensor", "S", tsq_hist[-1])
            if len(vv_hist) >= 2:
                kb.wait("tensor", "V", vv_hist[-2])
            for ic in range(8):
                lhs = ap3(xfm_b, xr * 1024 + ic * 128, [[1, 128]])
                w0 = ap3(wcat_b, ic * 1536, [[1, 512]])
                w1c = ap3(wcat_b, ic * 1536 + 512, [[1, 512]])
                w2c = ap3(wcat_b, ic * 1536 + 1024, [[1, 512]])
                st, sp = (ic == 0), (ic == 7)
                kb.emit("tensor", lambda t, l=lhs, r=w0, st=st, sp=sp, P=P: MM(t, ps[P][:], l, r, st, sp))
                kb.emit("tensor", lambda t, l=lhs, r=w1c, st=st, sp=sp, P=P: MM(t, ps[P + 1][:], l, r, st, sp))
                tv = kb.emit("tensor", lambda t, l=lhs, r=w2c, st=st, sp=sp, P=P: MM(t, ps[P + 2][:], l, r, st, sp),
                             inc=("T" if sp else None))
            t_after_proj[c] = tv
            kb.wait("vector", "T", tv)
            o_qt = ap3(qT_b, sub * 256, [[1, 256]])
            o_kt = ap3(kT_b, sub * 256, [[1, 256]])
            kb.emit("vector", lambda v, o=o_qt, P=P: v.tensor_scalar(o, ps[P][:, 0:256], 0.5, None, ALU.mult))
            kb.emit("vector", lambda v, o=o_kt, P=P: v.tensor_scalar(o, ps[P][:, 256:512], 0.35355339059327373, None, ALU.mult))
            o_v0 = ap3(vaug_b, sub * H * 65, [[65, 8], [1, 64]])
            o_v1 = ap3(vaug_b, sub * H * 65 + 8 * 65, [[65, 8], [1, 64]])
            kb.emit("vector", lambda v, o=o_v0, P=P: v.tensor_copy(o, ps[P + 1][:].rearrange("p (h e) -> p h e", h=8)))
            kb.emit("vector", lambda v, o=o_v1, P=P: v.tensor_copy(o, ps[P + 2][:].rearrange("p (h e) -> p h e", h=8)))
            o_vc = ap3(vaug_b, sub * H * 65 + 64, [[65, H]])
            i_vc = ap3(valid_s, c, [[0, H]])
            vv = kb.emit("vector", lambda v, o=o_vc, i=i_vc: v.tensor_copy(o, i), inc="V")
            v_after_evac[c] = vv
            vv_hist.append(vv)
            if c == 16:
                tap_now("qt_c16", ap3(qT_b, 0, [[1, 256]]), (128, 256), "V")
                tap_now("vaug_c16", ap3(vaug_b, 0, [[1, 1040]]), (128, 1040), "V")

            if own:
                kb.wait("tensor", "V", vv)
                for ti in range(2):
                    i_q = ap3(qT_b, sub * 256 + ti * 128, [[1, 128]])
                    i_k = ap3(kT_b, sub * 256 + ti * 128, [[1, 128]])
                    kb.emit("tensor", lambda t, i=i_q, ti=ti: t.transpose(
                        psb[3][:, ti * 128:(ti + 1) * 128], i, ident_b[:]))
                    tv2 = kb.emit("tensor", lambda t, i=i_k, ti=ti: t.transpose(
                        psb[3][:, 256 + ti * 128: 256 + (ti + 1) * 128], i, ident_b[:]),
                        inc=("T" if ti == 1 else None))
                kb.wait("scalar", "T", tv2)
                kb.wait("scalar", "SH", kb.now("SH"))
                o_qfm = ap3(qfm_b, sub * 128, [[512, 2], [1, 128]])
                o_kfm = ap3(kfm_b, sub * 128, [[512, 2], [1, 128]])
                kb.emit("scalar", lambda s_, o=o_qfm: s_.copy(
                    o, psb[3][:, 0:256].rearrange("p (ti t) -> p ti t", ti=2)))
                sv = kb.emit("scalar", lambda s_, o=o_kfm: s_.copy(
                    o, psb[3][:, 256:512].rearrange("p (ti t) -> p ti t", ti=2)), inc="S")
                kb.wait("tensor", "S", sv)
                kb.wait("sync", "S", sv)
                for src_t, dst_t in ((qfm_b, qfmS_b), (kfm_b, kfmS_b)):
                    for ti in range(2):
                        i_s = AP(src_t.tensor, src_t.offset + 16 * src_t.ap[0][0] + ti * 512 + sub * 128,
                                 [[src_t.ap[0][0], 112], [1, 128]])
                        o_s = AP(dst_t.tensor, dst_t.offset + ti * 512 + sub * 128,
                                 [[dst_t.ap[0][0], 112], [1, 128]])
                        kb.dma("sync", o_s, i_s, "SH")

        kb.mark(f"proj{bc}")
        kb.wait("vector", "V", v_after_evac[bc * NSUB + NSUB - 1])
        for sub in range(NSUB):
            c = bc * NSUB + sub
            kbase = sub * H * DE
            kt = ap3(kT_b, sub * 256, [[1, 256]])
            for (eng, h0) in (("vector", 0), ("gpsimd", 8)):
                in0 = ap3(kT_b, sub * 256 + h0 * FT, [[FT, 8], [1, FT], [0, FT]])
                in1 = ap3(kT_b, sub * 256 + h0 * FT, [[FT, 8], [0, FT], [1, FT]])
                outa = ap3(kft_b, kbase + h0 * DE, [[DE, 8], [FT, FT], [1, FT]])
                if eng == "gpsimd":
                    kb.wait("gpsimd", "T", t_after_ds)
                    kb.wait("gpsimd", "V", v_after_evac[c])
                kb.emit(eng, lambda e, o=outa, a=in0, b=in1: e.tensor_tensor(
                    o, a, b, ALU.mult))
            o_lin = ap3(kft_b, kbase + 256, [[DE, H], [1, FT]])
            i_lin = ap3(kT_b, sub * 256, [[FT, H], [1, FT]])
            kb.emit("vector", lambda v, o=o_lin, i=i_lin: v.tensor_scalar(
                o, i, 1.4142135623730951, None, ALU.mult))
            o_cst = ap3(kft_b, kbase + 272, [[DE, H]])
            i_cst = ap3(valid_s, c, [[0, H]])
            kb.emit("vector", lambda v, o=o_cst, i=i_cst: v.tensor_copy(o, i))
        v_after_gen = kb.emit("vector", lambda v: v.engine_nop(), inc="V")
        g_after_gen = kb.emit("gpsimd", lambda g: g.engine_nop(), inc="G")
        if bc == 4:
            kb.wait("gpsimd", "V", v_after_gen)
            tap_now("kft_c16", ap3(kft_b, 0, [[1, 4368]]), (128, 4368))

        kb.mark(f"gen{bc}")
        if own:
            for sub in range(NSUB):
                c = bc * NSUB + sub
                kb.wait("vector", "T", kb.now("T"))
                kb.wait("vector", "V", v_after_evac[c])
                for (eng, h0) in (("vector", 0), ("gpsimd", 8)):
                    in0 = ap3(qT_b, sub * 256 + h0 * FT, [[FT, 8], [1, FT], [0, FT]])
                    in1 = ap3(qT_b, sub * 256 + h0 * FT, [[FT, 8], [0, FT], [1, FT]])
                    outa = ap3(qfT_b, h0 * 256, [[256, 8], [FT, FT], [1, FT]])
                    if eng == "gpsimd":
                        kb.wait("gpsimd", "V", v_after_evac[c])
                        kb.wait("gpsimd", "T", kb.now("T"))
                    kb.emit(eng, lambda e, o=outa, a=in0, b=in1: e.tensor_tensor(o, a, b, ALU.mult))
                qv = kb.emit("vector", lambda v: v.engine_nop(), inc="V")
                qg = kb.emit("gpsimd", lambda g: g.engine_nop(), inc="G")
                kb.wait("tensor", "V", qv)
                kb.wait("tensor", "G", qg)
                grp = [(h, cc) for h in range(H) for cc in range(2)]
                for gi in range(0, 32, 4):
                    pbank = psb[4] if (gi // 4) % 2 == 0 else psb[3]
                    if len(sv2_hist) >= 2:
                        kb.wait("tensor", "S", sv2_hist[-2])
                    for j in range(4):
                        h, cc = grp[gi + j]
                        i_t = ap3(qfT_b, h * 256 + cc * 128, [[1, 128]])
                        tv3 = kb.emit("tensor", lambda t, i=i_t, j=j, pb=pbank: t.transpose(
                            pb[:, j * 128:(j + 1) * 128], i, ident_b[:]),
                            inc=("T" if j == 3 else None))
                    kb.wait("scalar", "T", tv3)
                    for j in range(4):
                        h, cc = grp[gi + j]
                        o_q = ap3(qfx2_b, (sub * H + h) * 256 + cc * 128, [[1, 128]])
                        sv2 = kb.emit("scalar", lambda s_, j=j, o=o_q, pb=pbank: s_.copy(
                            o, pb[:, j * 128:(j + 1) * 128]),
                            inc=("S" if j == 3 else None))
                    sv2_hist.append(sv2)
                if sub == 0 and bc == 4:
                    kb.wait("gpsimd", "S", sv2_hist[-1])
                    tap_now("qfx2_c16", ap3(qfx2_b, 0, [[1, 512]]), (128, 512), "S")

            kb.emit("vector", lambda v: v.tensor_copy(s0b, s0f))
            kb.emit("vector", lambda v: v.tensor_copy(s1b, s1f))
            kb.emit("vector", lambda v: v.tensor_copy(s2b, s2f))
            scast = kb.emit("vector", lambda v: v.tensor_copy(scb, scf), inc="V")
            kb.wait("tensor", "V", scast)
            kb.wait("tensor", "SH", kb.now("SH"))
            kb.mark(f"scast{bc}")
            kb.wait("tensor", "S", sv2_hist[-1])
            for ti in range(2):
                for ps_ in range(2):
                    sbank = 2 if (ti * 2 + ps_) % 2 == 0 else 0
                    if len(srep_hist) >= 2:
                        kb.wait("tensor", "S", srep_hist[-2])
                    for j in range(4):
                        jj = ps_ * 4 + j
                        h = ti * 8 + jj
                        l_sel = sel16_b[:, jj * 128:(jj + 1) * 128]
                        r_s2 = ap3(s2b, h * 65, [[1, 65]])
                        trep = kb.emit("tensor", lambda t, l=l_sel, r=r_s2, j=j, sb_=sbank: MM(
                            t, ps[sb_][:, j * 65:(j + 1) * 65], l, r, j == 0, j == 3),
                            inc=("T" if j == 3 else None))
                    kb.wait("scalar", "T", trep)
                    o_st = ap3(s2stk, ti * 520 + ps_ * 260, [[1, 260]])
                    srep = kb.emit("scalar", lambda s_, o=o_st, sb_=sbank: s_.copy(
                        o, ps[sb_][:, 0:260]), inc="S")
                    srep_hist.append(srep)
            kb.wait("tensor", "S", srep_hist[-1])

            kb.mark(f"qfx{bc}")
            kb.mark(f"rep{bc}")
            kb.wait("tensor", "V", vv_hist[-1])
            for pair in range(2):
                qcs = (0, 1) if pair == 0 else (2, 3)
                qhi = (qcs[-1] + 1) * 128

                def num_ap(h, qi):
                    if h < 7:
                        bank = 3 if qi == 0 else 5
                        return ps[bank][:, h * 66:h * 66 + 65]
                    if h < 14:
                        bank = 4 if qi == 0 else 6
                        return ps[bank][:, (h - 7) * 66:(h - 7) * 66 + 65]
                    return ps[7][:, qi * 132 + (h - 14) * 66: qi * 132 + (h - 14) * 66 + 65]

                strip_off = {}
                soff = 0
                for kc in range(qcs[-1] + 1):
                    strip_off[kc] = soff
                    soff += qhi - max(kc, qcs[0]) * 128
                assert soff <= 1024

                for h in range(H):
                    ti, hh = h // 8, h % 8
                    if hh % 2 == 0:
                        base = hh * 16
                        kfm_t, qfm_t = kfm_b, qfm_b
                    else:
                        base = hh * 16 - 16
                        kfm_t, qfm_t = kfmS_b, qfmS_b
                    tp = (96, 0) if base == 96 else None
                    tat = None
                    for kc in range(qcs[-1] + 1):
                        par = len(tsq_hist) % 2
                        sbank = 2 if par == 0 else 0
                        sqo = par * 256
                        if len(tsq_hist) >= 2:
                            kb.wait("tensor", "S", tsq_hist[-2])
                        if len(tat_hist) >= 2:
                            kb.wait("scalar", "V", tat_hist[-2])
                        sp_lo = max(kc, qcs[0]) * 128
                        nq = qhi - sp_lo
                        l_sc = AP(kfm_t.tensor, kfm_t.offset + base * kfm_t.ap[0][0] + ti * 512 + kc * 128,
                                  [[kfm_t.ap[0][0], 16], [1, 128]])
                        r_sc = AP(qfm_t.tensor, qfm_t.offset + base * qfm_t.ap[0][0] + ti * 512 + sp_lo,
                                  [[qfm_t.ap[0][0], 16], [1, nq]])
                        tsc = kb.emit("tensor", lambda t, l=l_sc, r=r_sc, nq=nq, tp=tp, sbank=sbank:
                                      MM(t, ps[sbank][:, 0:nq], l, r, True, True, tp), inc="T")
                        kb.wait("scalar", "T", tsc)
                        tsq = kb.emit("scalar", lambda s_, nq=nq, sbank=sbank, sqo=sqo: s_.activation(
                            sq_b[:, sqo:sqo + nq], ps[sbank][:, 0:nq], AF.Square, bias=1.0,
                            scale=1.4142135623730951), inc="S")
                        tsq_hist.append(tsq)
                        kb.wait("vector", "S", tsq)
                        has_diag = (sp_lo == kc * 128)
                        so = strip_off[kc]

                        def poly(v, nq=nq, has_diag=has_diag, so=so, sqo=sqo):
                            ins = None
                            col = 0
                            if has_diag:
                                ins = v.scalar_tensor_tensor(atm_b[:, so:so + 128], sq_b[:, sqo:sqo + 128], 1.0,
                                                             maskh_b[:], ALU.add, ALU.mult)
                                col = 128
                            if col < nq:
                                ins = v.tensor_scalar(atm_b[:, so + col:so + nq], sq_b[:, sqo + col:sqo + nq],
                                                      0.5, 0.5, ALU.mult, ALU.add)
                            return ins
                        tat = kb.emit("vector", poly, inc="V")
                        tat_hist.append(tat)
                    kb.wait("tensor", "V", tat)
                    last_mm = None
                    for qi, qc in enumerate(qcs):
                        o = num_ap(h, qi)
                        mms = []
                        lx0 = ap3(qfx2_b, (qc * H + h) * 256, [[1, 128]])
                        lx1 = ap3(qfx2_b, (qc * H + h) * 256 + 128, [[1, 128]])
                        mms.append((lx0, ap3(s0b, h * 65, [[1, 65]])))
                        mms.append((lx1, ap3(s1b, h * 65, [[1, 65]])))
                        lin = ap3(qfm_b, ti * 512 + qc * 128, [[1, 128]])
                        mms.append((lin, ap3(s2stk, ti * 520 + hh * 65, [[1, 65]])))
                        mms.append((ones_row[:, :], ap3(scb, h * 65, [[1, 65]])))
                        for kc in range(qc + 1):
                            so = strip_off[kc] + (qc * 128 - max(kc, qcs[0]) * 128)
                            mms.append((atm_b[:, so:so + 128],
                                        ap3(vaug_b, kc * H * 65 + h * 65, [[1, 65]])))
                        if no_inter:
                            mms = mms[3:]
                        for mi, (l_m, r_m) in enumerate(mms):
                            last_mm = kb.emit("tensor", lambda t, o=o, l=l_m, r=r_m,
                                              st=(mi == 0), sp=(mi == len(mms) - 1):
                                              MM(t, o, l, r, st, sp),
                                              inc=("T" if mi == len(mms) - 1 else None))
                    kb.wait("vector", "T", last_mm)
                tnum = kb.emit("tensor", lambda t: t.nop(), inc="T")
                kb.wait("vector", "T", tnum)
                if bc == 4 and pair == 0 and ("num_a" in taps or "num_7" in taps):
                    kb.wait("scalar", "T", tnum)
                    sda = kb.emit("scalar", lambda s_: s_.copy(dump_b[:, 0:462], ps[3][:, 0:462]), inc="S")
                    kb.wait("gpsimd", "S", sda)
                    tap_now("num_a", dump_b[:, 0:462], (128, 462))
                    sdb = kb.emit("scalar", lambda s_: s_.copy(dump_b[:, 0:264], ps[7][:, 0:264]), inc="S")
                    kb.wait("gpsimd", "S", sdb)
                    tap_now("num_7", dump_b[:, 0:264], (128, 264))
                for qi, qc in enumerate(qcs):
                    bA = 3 if qi == 0 else 5
                    bB = 4 if qi == 0 else 6
                    rq = 0
                    if qi == 1:
                        kb.wait("vector", "V", yv)
                    kb.emit("vector", lambda v, bA=bA, rq=rq: v.tensor_scalar(
                        recip_s[:, rq:rq + 7], ap3(ps[bA], 64, [[66, 7]]), EPS_ATTN, None, ALU.add))
                    kb.emit("vector", lambda v, bB=bB, rq=rq: v.tensor_scalar(
                        recip_s[:, rq + 7:rq + 14], ap3(ps[bB], 64, [[66, 7]]), EPS_ATTN, None, ALU.add))
                    tdn = kb.emit("vector", lambda v, qi=qi, rq=rq: v.tensor_scalar(
                        recip_s[:, rq + 14:rq + 16], ap3(ps[7], qi * 132 + 64, [[66, 2]]), EPS_ATTN, None, ALU.add),
                        inc="V")
                    kb.wait("vector", "V", tdn)
                    trc = kb.emit("vector", lambda v, rq=rq: v.reciprocal(
                        recip_s[:, rq:rq + 16], recip_s[:, rq:rq + 16]), inc="V")
                    kb.wait("vector", "V", trc)
                    trip = ((ap3(y_b, qc * D, [[64, 7], [1, 64]]),
                             ap3(ps[bA], 0, [[66, 7], [1, 64]]),
                             ap3(recip_s, rq, [[1, 7], [0, 64]])),
                            (ap3(y_b, qc * D + 7 * 64, [[64, 7], [1, 64]]),
                             ap3(ps[bB], 0, [[66, 7], [1, 64]]),
                             ap3(recip_s, rq + 7, [[1, 7], [0, 64]])),
                            (ap3(y_b, qc * D + 14 * 64, [[64, 2], [1, 64]]),
                             ap3(ps[7], qi * 132, [[66, 2], [1, 64]]),
                             ap3(recip_s, rq + 14, [[1, 2], [0, 64]])))
                    for o_y, i_n, r_r in trip:
                        yv = kb.emit("vector", lambda v, o=o_y, i=i_n, r=r_r:
                                     v.scalar_tensor_tensor(o, i, 1.0, r, ALU.mult, ALU.mult),
                                     inc="V")
                    if bc == 4 and pair == 0 and qi == 0 and "rcp_q0" in taps:
                        tap_now("rcp_q0", recip_s[:, 0:16], (128, 16), "V")
                kb.wait("tensor", "V", yv)
                if bc == 4 and pair == 0 and "y_p0" in taps:
                    tap_now("y_p0", ap3(y_b, 0, [[1, 2048]]), (128, 2048), "V")
            if bc == 4:
                tap_now("y_b4", ap3(y_b, 0, [[1, 4096]]), (128, 4096), "V")

        kb.mark(f"scan{bc}")
        kb.wait("tensor", "V", v_after_gen)
        kb.wait("tensor", "G", g_after_gen)
        wave_id = 0
        sv3_hist = []
        for fc in range(4):
            for h0, hn in ((0, 7), (7, 7), (14, 2)):
                bank = 3 if (wave_id % 2 == 0) else 4
                wave_id += 1
                if len(sv3_hist) >= 2:
                    kb.wait("tensor", "V", sv3_hist[-2])
                tds = None
                for sub in range(NSUB):
                    for j in range(hn):
                        h = h0 + j
                        if fc < 2:
                            lhs = ap3(kft_b, (sub * H + h) * DE + fc * 128, [[1, 128]])
                            o = ps[bank][:, j * 66:j * 66 + 65]
                        elif fc == 2:
                            lhs = ap3(kft_b, (sub * H + h) * DE + 256, [[1, 16]])
                            o = ps[bank][0:16, j * 66:j * 66 + 65]
                        else:
                            lhs = ap3(kft_b, (sub * H + h) * DE + 272, [[1, 1]])
                            o = ps[bank][0:1, j * 66:j * 66 + 65]
                        r_v = ap3(vaug_b, sub * H * 65 + h * 65, [[1, 65]])
                        st = (sub == 0 and j == 0)
                        sp = (sub == NSUB - 1 and j == hn - 1)
                        tds = kb.emit("tensor", lambda t, o=o, l=lhs, r=r_v, st=st, sp=sp:
                                      MM(t, o, l, r, st, sp), inc=("T" if sp else None))
                kb.wait("vector", "T", tds)
                if fc < 2:
                    dst = s0f if fc == 0 else s1f
                    i_p = ap3(ps[bank], 0, [[66, hn], [1, 65]])
                    o_s3 = ap3(dst, h0 * 65, [[65, hn], [1, 65]])
                elif fc == 2:
                    dst = s2f
                    i_p = AP(ps[bank].tensor, ps[bank].offset, [[ps[bank].ap[0][0], 16], [66, hn], [1, 65]])
                    o_s3 = AP(s2f.tensor, s2f.offset + h0 * 65, [[s2f.ap[0][0], 16], [65, hn], [1, 65]])
                else:
                    i_p = AP(ps[bank].tensor, ps[bank].offset, [[ps[bank].ap[0][0], 1], [66, hn], [1, 65]])
                    o_s3 = AP(scf.tensor, scf.offset + h0 * 65, [[scf.ap[0][0], 1], [65, hn], [1, 65]])
                sv3 = kb.emit("vector", lambda v, o=o_s3, i=i_p: v.tensor_tensor(
                    o, o, i, ALU.add), inc="V")
                sv3_hist.append(sv3)
        kb.wait("tensor", "V", sv3_hist[-1])
        t_after_ds = kb.now("T")

        kb.mark(f"state{bc}")
        if own:
            for sub in range(NSUB):
                co = (bc - NBIG // 2) * NSUB + sub
                xr = sub % 2
                if co - 2 in v_after_h:
                    kb.wait("sync", "V", v_after_h[co - 2])
                xts = "XT0" if co % 2 == 0 else "XT1"
                dxt = kb.dma("sync", xtok_b[:, xr * D:(xr + 1) * D],
                             dram["x_tok"][co * 128:(co + 1) * 128, :], xts)
                for j4 in range(2):
                    for j in range(4):
                        jj = j4 * 4 + j
                        i_y = ap3(y_b, sub * D + jj * 128, [[1, 128]])
                        tv4 = kb.emit("tensor", lambda t, i=i_y, j=j: t.transpose(
                            psb[3][:, j * 128:(j + 1) * 128], i, ident_b[:]),
                            inc=("T" if j == 3 else None))
                    kb.wait("scalar", "T", tv4)
                    o_yt = ap3(yT_b, j4 * 512, [[1, 512]])
                    sv4 = kb.emit("scalar", lambda s_, o=o_yt: s_.copy(o, psb[3][:, 0:512]), inc="S")
                    kb.wait("tensor", "S", sv4)
                for ic in range(8):
                    st, sp = (ic == 0), (ic == 7)
                    l_y = ap3(yT_b, ic * 128, [[1, 128]])
                    kb.emit("tensor", lambda t, l=l_y, ic=ic, st=st, sp=sp: MM(
                        t, ps[5][:], l, ap3(wo_b, ic * D, [[1, 512]]), st, sp))
                    two = kb.emit("tensor", lambda t, l=l_y, ic=ic, st=st, sp=sp: MM(
                        t, ps[6][:], l, ap3(wo_b, ic * D + 512, [[1, 512]]), st, sp),
                        inc=("T" if sp else None))
                kb.wait("vector", "T", two)
                kb.wait("vector", xts, dxt)
                if co - 2 in ds_after_x1:
                    kb.wait("vector", "X1W0" if (co - 2) % 2 == 0 else "X1W1", ds_after_x1[co - 2])
                hr = sub % 2
                o_h0 = h_b[:, hr * D: hr * D + 512]
                o_h1 = h_b[:, hr * D + 512: (hr + 1) * D]
                kb.emit("vector", lambda v, o=o_h0, xr=xr: v.tensor_tensor(
                    o, xtok_b[:, xr * D: xr * D + 512], ps[5][:], ALU.add))
                hv = kb.emit("vector", lambda v, o=o_h1, xr=xr: v.tensor_tensor(
                    o, xtok_b[:, xr * D + 512:(xr + 1) * D], ps[6][:], ALU.add), inc="V")
                v_after_h[co] = hv
                kb.wait("tensor", "V", hv)
                if co == 0:
                    tap_now("h_c0", h_b[:, 0:1024], (128, 1024), "V")
                vap = ln_tile(h_b[:, hr * D:(hr + 1) * D], 0,
                              h_b[:, hr * D:(hr + 1) * D], None)
                kb.wait("sync", "V", vap)
                ds_after_x1[co] = kb.dma("sync", x1_scr[co * 128:(co + 1) * 128, :],
                                         h_b[:, hr * D:(hr + 1) * D],
                                         "X1W0" if co % 2 == 0 else "X1W1")

    kb.mark("attn")
    kb.wait("sync", "X1W0", kb.now("X1W0"))
    kb.wait("sync", "X1W1", kb.now("X1W1"))
    t_x1tr = {}
    vx1_by_co = {}
    for co in range(16):
        xr = co % 2
        x1s = "X1R0" if co % 2 == 0 else "X1R1"
        if co - 2 in vx1_by_co:
            kb.wait("sync", "V", vx1_by_co[co - 2])
        dx1 = kb.dma("sync", xtok_b[:, xr * D:(xr + 1) * D],
                     x1_scr[co * 128:(co + 1) * 128, :], x1s)
        kb.wait("vector", x1s, dx1)
        if co - 2 in t_x1tr:
            kb.wait("vector", "T", t_x1tr[co - 2])
        vx1 = kb.emit("vector", lambda v, xr=xr: v.tensor_copy(
            x1bf_b[:, xr * D:(xr + 1) * D], xtok_b[:, xr * D:(xr + 1) * D]), inc="V")
        vx1_by_co[co] = vx1
        kb.wait("tensor", "V", vx1)
        for j4 in range(2):
            for j in range(4):
                jj = j4 * 4 + j
                i_x = x1bf_b[:, xr * D + jj * 128: xr * D + (jj + 1) * 128]
                tvx = kb.emit("tensor", lambda t, i=i_x, j=j: t.transpose(
                    psb[3][:, j * 128:(j + 1) * 128], i, ident_b[:]),
                    inc=("T" if j == 3 else None))
            t_x1tr[co] = tvx
            kb.wait("scalar", "T", tvx)
            o_fm = ap3(x1fm_b, (j4 * 4) * OWN + co * 128, [[OWN, 4], [1, 128]])
            svx = kb.emit("scalar", lambda s_, o=o_fm: s_.copy(
                o, psb[3][:, 0:512].rearrange("p (g t) -> p g t", g=4)), inc="S")
            kb.wait("tensor", "S", svx)

    add_tap("x1fm", ap3(x1fm_b, 0, [[1, 2048]]), (128, 2048))

    kb.mark("x1fm")
    t_after_moe_e = {}
    t_after_swg = {}
    t_after_sw2 = {}
    v_after_out = {}
    do_after_out = {}
    v_x1bf_last = kb.now("V")
    for oc in range(8):
        o_i = ap3(omoe_b, oc * 2048, [[8, 256], [1, 8]])
        i_i = ap3(b2p_s, oc * 8, [[0, 256], [1, 8]])
        kb.emit("vector", lambda v, o=o_i, i=i_i: v.tensor_copy(o, i))
    vini = kb.emit("vector", lambda v: v.engine_nop(), inc="V")
    kb.wait("tensor", "V", vini)
    tap_now("om_init", ap3(omoe_b, 0, [[1, 1024]]), (128, 1024), "V")
    sh_hist = []
    vw2_hist = []
    for e in range(NE):
        if e - 1 in t_after_moe_e:
            kb.wait("gpsimd", "T", t_after_moe_e[e - 1])
        else:
            kb.wait("gpsimd", "T", kb.now("T"))
        dw1 = kb.dma("gpsimd", ap3(w1_b, 0, [[D, 8], [1, D]]),
                     dram["w1"][e].rearrange("(c p) n -> p c n", p=128), "WM1")
        dw2 = kb.dma("gpsimd", ap3(w2_b, 0, [[D, 8], [1, D]]),
                     dram["w2"][e].rearrange("(c p) n -> p c n", p=128), "WM2")
        lo, delta = MOE_RES[e]
        kb.wait("tensor", "WM1", dw1)
        kb.wait("scalar", "T", kb.now("T"))
        for hf in range(8):
            bank = hf % 2
            if len(sh_hist) >= 2:
                kb.wait("tensor", "S", sh_hist[-2])
            for ic in range(8):
                rhs = ap3(x1fm_b, ic * OWN + lo, [[8, 256], [delta, 2]])
                l_w = ap3(w1_b, ic * D + hf * 128, [[1, 128]])
                o_h = ps[bank][:, 0:512].rearrange("p (g r) -> p g r", g=256)
                th = kb.emit("tensor", lambda t, o=o_h, l=l_w, r=rhs, ic=ic:
                             MM(t, o, l, r, ic == 0, ic == 7), inc=("T" if ic == 7 else None))
            kb.wait("scalar", "T", th)
            sh = kb.emit("scalar", lambda s_, e=e, hf=hf, bank=bank: s_.activation(
                hmoe_b[:, hf * 512:(hf + 1) * 512], ps[bank][:, 0:512], AF.Gelu,
                bias=b1_s[:, e * 8 + hf: e * 8 + hf + 1], scale=1.0), inc="S")
            sh_hist.append(sh)
        if e == 0:
            kb.wait("gpsimd", "S", sh_hist[-1])
            tap_now("hm_e0", hmoe_b[:, 0:2048], (128, 2048), "S")
        kb.wait("tensor", "WM2", dw2)
        kb.wait("tensor", "S", sh_hist[-1])
        for oc in range(8):
            bank = 2 + oc % 2
            if len(vw2_hist) >= 2:
                kb.wait("tensor", "V", vw2_hist[-2])
            for hf in range(8):
                l_w = ap3(w2_b, hf * D + oc * 128, [[1, 128]])
                tw2 = kb.emit("tensor", lambda t, l=l_w, hf=hf, bank=bank: MM(
                    t, ps[bank][:, 0:512], l, hmoe_b[:, hf * 512:(hf + 1) * 512],
                    hf == 0, hf == 7), inc=("T" if hf == 7 else None))
            kb.wait("vector", "T", tw2)
            o_m = ap3(omoe_b, oc * 2048 + lo, [[8, 256], [delta, 2]])
            i_m = ps[bank][:, 0:512].rearrange("p (g r) -> p g r", g=256)
            vw2 = kb.emit("vector", lambda v, o=o_m, i=i_m: v.scalar_tensor_tensor(
                o, i, 0.25, o, ALU.mult, ALU.add), inc="V")
            vw2_hist.append(vw2)
        t_after_moe_e[e] = kb.now("T")
    tap_now("om_moe", ap3(omoe_b, 0, [[1, 1024]]), (128, 1024), "V")

    kb.mark("moe")
    kb.wait("tensor", "S", sh_hist[-1])
    kb.wait("tensor", "V", vw2_hist[-1])
    tsg_hist = []
    tf_hist = []
    vtf_hist = []
    sq2_b = bfv("sq2", 512)
    for half in range(2):
        tok0 = half * 1024
        for g in range(4):
            gr = g % 2
            key = (half, g - 2)
            if key in t_after_swg:
                kb.wait("gpsimd", "T", t_after_swg[key])
            elif g < 2:
                kb.wait("gpsimd", "T",
                        t_after_moe_e[NE - 1] if half == 0 else t_after_swg[(0, 3)])
            sws = "SWE" if g % 2 == 0 else "SWO"
            dsw1 = kb.dma("gpsimd", ap3(sw1_b, gr * 4096, [[512, 8], [1, 512]]),
                          dram["sw1"][:, g * 512:(g + 1) * 512].rearrange("(c p) n -> p c n", p=128), sws)
            dsw3 = kb.dma("gpsimd", ap3(sw3_b, gr * 4096, [[512, 8], [1, 512]]),
                          dram["sw3"][:, g * 512:(g + 1) * 512].rearrange("(c p) n -> p c n", p=128), sws)
            kb.wait("tensor", sws, dsw3)
            for hfl in range(4):
                hf = g * 4 + hfl
                for tt in range(2):
                    bg = tt % 2
                    bu = 2 + tt % 2
                    if len(tsg_hist) >= 2:
                        kb.wait("tensor", "V", tsg_hist[-2])
                        kb.wait("scalar", "V", tsg_hist[-2])
                    for ic in range(8):
                        rhs = ap3(x1fm_b, ic * OWN + tok0 + tt * 512, [[1, 512]])
                        l1 = ap3(sw1_b, gr * 4096 + ic * 512 + hfl * 128, [[1, 128]])
                        l3 = ap3(sw3_b, gr * 4096 + ic * 512 + hfl * 128, [[1, 128]])
                        kb.emit("tensor", lambda t, l=l1, r=rhs, ic=ic, bg=bg:
                                MM(t, ps[bg][:], l, r, ic == 0, ic == 7))
                        tgu = kb.emit("tensor", lambda t, l=l3, r=rhs, ic=ic, bu=bu:
                                      MM(t, ps[bu][:], l, r, ic == 0, ic == 7),
                                      inc=("T" if ic == 7 else None))
                    kb.wait("scalar", "T", tgu)
                    sq_t = sq_b[:, 0:512] if tt == 0 else sq2_b[:, 0:512]
                    du_t = dump_b[:, 0:512] if tt == 0 else dump_b[:, 512:1024]
                    tsi = kb.emit("scalar", lambda s_, hf=hf, bg=bg, o=sq_t: s_.activation(
                        o, ps[bg][:], AF.Silu,
                        bias=sb1_s[:, hf:hf + 1], scale=1.0), inc="S")
                    kb.wait("vector", "S", tsi)
                    kb.emit("vector", lambda v, hf=hf, bu=bu, o=du_t: v.tensor_scalar(
                        o, ps[bu][:], sb3_s[:, hf:hf + 1], 0.5, ALU.add, ALU.mult))
                    o_sg = ap3(sgate_b, hf * 1024 + tt * 512, [[1, 512]])
                    tsg = kb.emit("vector", lambda v, o=o_sg, a=sq_t, b=du_t: v.tensor_tensor(
                        o, a, b, ALU.mult), inc="V")
                    tsg_hist.append(tsg)
                    if half == 0 and hf == 0 and tt == 1:
                        kb.wait("tensor", "V", tsg)
                        tap_now("sg_0", ap3(sgate_b, 0, [[1, 1024]]), (128, 1024), "V")
            t_after_swg[(half, g)] = kb.now("T")
        kb.mark(f"sg{half}")
        kb.wait("tensor", "V", tsg_hist[-1])
        for oc in range(8):
            ocr = oc % 2
            key = (half, oc - 2)
            if key in t_after_sw2:
                kb.wait("gpsimd", "T", t_after_sw2[key])
            elif oc < 2:
                kb.wait("gpsimd", "T",
                        t_after_moe_e[NE - 1] if half == 0 else t_after_sw2[(0, 7)])
            w2sem = "W2E" if oc % 2 == 0 else "W2O"
            dsw2 = kb.dma("gpsimd", ap3(sw2_b, ocr * 2048, [[128, 16], [1, 128]]),
                          dram["sw2"][:, oc * 128:(oc + 1) * 128].rearrange("(c p) n -> p c n", p=128), w2sem)
            kb.wait("tensor", w2sem, dsw2)
            for tt in range(2):
                bank = 4 + tt % 2
                if len(tf_hist) >= 2:
                    kb.wait("tensor", "V", tf_hist[-2])
                for hf in range(16):
                    l_w = ap3(sw2_b, ocr * 2048 + hf * 128, [[1, 128]])
                    r_s = ap3(sgate_b, hf * 1024 + tt * 512, [[1, 512]])
                    tsw2 = kb.emit("tensor", lambda t, l=l_w, r=r_s, hf=hf, bank=bank:
                                   MM(t, ps[bank][:], l, r, hf == 0, hf == 15),
                                   inc=("T" if hf == 15 else None))
                kb.wait("vector", "T", tsw2)
                o_f = ap3(omoe_b, oc * 2048 + tok0 + tt * 512, [[1, 512]])
                tf = kb.emit("vector", lambda v, o=o_f, oc=oc, bank=bank: v.scalar_tensor_tensor(
                    o, ps[bank][:], sb2h_s[:, oc:oc + 1], o, ALU.add, ALU.add), inc="V")
                tf_hist.append(tf)
            if half == 0 and oc == 0:
                tap_now("f_0", ap3(omoe_b, 0, [[1, 1024]]), (128, 1024), "V")
            t_after_sw2[(half, oc)] = kb.now("T")
        kb.mark(f"sw2{half}")
        kb.wait("tensor", "V", tf_hist[-1])
        for ct in range(8):
            co = half * 8 + ct
            xr = ct % 2
            if co - 2 in v_after_out:
                kb.wait("sync", "V", v_after_out[co - 2])
            elif half == 0 and ct < 2:
                kb.wait("sync", "V", v_x1bf_last)
            x1s = "X1R0" if co % 2 == 0 else "X1R1"
            dx1 = kb.dma("sync", xtok_b[:, xr * D:(xr + 1) * D],
                         x1_scr[co * 128:(co + 1) * 128, :], x1s)
            kb.wait("vector", x1s, dx1)
            hr = ct % 2
            for j in range(8):
                bank = 6 + j % 2
                if len(vtf_hist) >= 2:
                    kb.wait("tensor", "V", vtf_hist[-2])
                i_f = ap3(omoe_b, j * 2048 + co * 128, [[1, 128]])
                tfj = kb.emit("tensor", lambda t, i=i_f, bank=bank: t.transpose(
                    ps[bank][:, 0:128], i, ident_f[:]), inc="T")
                kb.wait("vector", "T", tfj)
                o_h = h_b[:, hr * D + j * 128: hr * D + (j + 1) * 128]
                i_x = xtok_b[:, xr * D + j * 128: xr * D + (j + 1) * 128]
                vtf = kb.emit("vector", lambda v, o=o_h, i=i_x, bank=bank: v.tensor_tensor(
                    o, i, ps[bank][:, 0:128], ALU.add), inc="V")
                vtf_hist.append(vtf)
            if co - 2 in do_after_out:
                kb.wait("vector", "DO0" if (co - 2) % 2 == 0 else "DO1", do_after_out[co - 2])
            if half == 0 and ct == 0:
                tap_now("h2_0", h_b[:, 0:1024], (128, 1024), "V")
            vo = ln_tile(h_b[:, hr * D:(hr + 1) * D], 1,
                         x2o_b[:, xr * D:(xr + 1) * D], None)
            v_after_out[co] = vo
            kb.wait("sync", "V", vo)
            do_after_out[co] = kb.dma("sync", out_d[co * 128:(co + 1) * 128, :],
                                      x2o_b[:, xr * D:(xr + 1) * D],
                                      "DO0" if co % 2 == 0 else "DO1")

    for t, src_ap in tap_dmas:
        kb.dma("gpsimd", t, src_ap, "DTAP")
    kb.wait("sync", "DO0", kb.now("DO0"))
    kb.wait("sync", "DO1", kb.now("DO1"))
    kb.wait("sync", "DTAP", kb.now("DTAP"))

    if cut is not None:
        kb.cut_to(cut)
        kb.sem("DCUT")
        for co in range(16):
            kb.dma("sync", out_d[co * 128:(co + 1) * 128, :],
                   x2o_b[:, (co % 2) * D:((co % 2) + 1) * D], "DCUT")
        kb.wait("sync", "DCUT", kb.now("DCUT"))

    with ctx:
        with nc.Block() as block:
            @block.sync
            def _(sync):
                for f in kb.ops["sync"]:
                    f(sync)

            @block.tensor
            def _(tns):
                for f in kb.ops["tensor"]:
                    f(tns)

            @block.vector
            def _(vec):
                for f in kb.ops["vector"]:
                    f(vec)

            @block.scalar
            def _(scl):
                for f in kb.ops["scalar"]:
                    f(scl)

            @block.gpsimd
            def _(gps):
                for f in kb.ops["gpsimd"]:
                    f(gps)
    return nc


def stage_inputs(inputs):
    x = np.ascontiguousarray(np.asarray(inputs["L0_attn_hidden_states"], dtype=np.float32))
    Wq = np.asarray(inputs["L0_attn_Wq"], np.float32)
    Wk = np.asarray(inputs["L0_attn_Wk"], np.float32)
    Wv = np.asarray(inputs["L0_attn_Wv"], np.float32)
    Wo = np.asarray(inputs["L0_attn_Wo"], np.float32)
    wcat = np.ascontiguousarray(np.concatenate([Wq.T, Wk.T, Wv.T], axis=1))
    wo_in = np.ascontiguousarray(Wo.T)
    sw1 = np.ascontiguousarray(np.asarray(inputs["L0_ffn_sw1"], np.float32))
    sw3 = np.ascontiguousarray(np.asarray(inputs["L0_ffn_sw3"], np.float32))
    sw2 = np.ascontiguousarray(np.asarray(inputs["L0_ffn_sw2"], np.float32))
    sb1 = np.ascontiguousarray(np.asarray(inputs["L0_ffn_sb1"], np.float32).reshape(16, 128).T)
    sb3 = np.ascontiguousarray(np.asarray(inputs["L0_ffn_sb3"], np.float32).reshape(16, 128).T)
    sb2 = np.asarray(inputs["L0_ffn_sb2"], np.float32)
    sb2h = np.ascontiguousarray((0.5 * sb2).reshape(8, 128).T)
    w1 = np.ascontiguousarray(np.asarray(inputs["L0_ffn_w1"], np.float32))
    b1 = np.asarray(inputs["L0_ffn_b1"], np.float32)
    b1r = np.ascontiguousarray(b1.reshape(NE, 8, 128).transpose(2, 0, 1))
    w2 = np.ascontiguousarray(np.asarray(inputs["L0_ffn_w2"], np.float32))
    b2 = np.asarray(inputs["L0_ffn_b2"], np.float32)
    b2p = np.zeros((128, 8, 8), np.float32)
    for r in range(8):
        e0 = (3 * r) % 8
        e1 = (3 * r + 7) % 8
        b2p[:, :, r] = (0.25 * (b2[e0] + b2[e1])).reshape(8, 128).T
    maskh = np.ascontiguousarray(np.triu(np.ones((128, 128), np.float32)) * 0.5)
    ident = np.eye(128, dtype=np.float32)
    rep4 = np.zeros((16, 128), np.float32)
    for g in range(4):
        rep4[np.arange(16), 32 * g + np.arange(16)] = 1.0
    sel16 = np.zeros((16, 1024), np.float32)
    for j in range(8):
        sel16[np.arange(16), j * 128 + 16 * j + np.arange(16)] = 1.0

    per_core = []
    for b in range(4):
        for half in range(2):
            if half == 0:
                win = np.concatenate([np.zeros((2048, D), np.float32), x[b, :2048]], axis=0)
                val = np.concatenate([np.zeros(16, np.float32), np.ones(16, np.float32)])
            else:
                win = x[b]
                val = np.ones(32, np.float32)
            per_core.append({
                "x_fm": np.ascontiguousarray(win.T),
                "x_tok": np.ascontiguousarray(x[b, half * 2048:(half + 1) * 2048]),
                "valid": np.ascontiguousarray(np.broadcast_to(val[None, :], (128, NCH))),
                "wcat": wcat, "wo_in": wo_in,
                "sw1": sw1, "sw3": sw3, "sw2": sw2,
                "sb1": sb1, "sb3": sb3, "sb2h": sb2h,
                "w1": w1, "b1": b1r, "b2p": b2p, "w2": w2,
                "maskh": maskh, "ident": ident, "rep4": rep4, "sel16": sel16,
            })
    return per_core


def kernel(**inputs):
    per_core = stage_inputs(inputs)
    nc = build_program()
    res = run_bass_kernel_spmd(nc, per_core, core_ids=list(range(8)))
    out = np.empty((4, 4096, 1024), np.float32)
    for b in range(4):
        for half in range(2):
            out[b, half * 2048:(half + 1) * 2048] = res.results[b * 2 + half]["out"]
    return out
